# version 32
# baseline (speedup 1.0000x reference)
import math
import numpy as np
import ml_dtypes
import concourse.bass as bass
import concourse.mybir as mybir
from concourse.bass_utils import run_bass_kernel_spmd

F32 = mybir.dt.float32
BF16 = mybir.dt.bfloat16
AF = mybir.ActivationFunctionType
ALU = mybir.AluOpType
AX = mybir.AxisListType
BF = ml_dtypes.bfloat16

Bb_, T_, C_ = 4, 2048, 2048
NCORE = 8
NH, NKV, HS = 16, 4, 128
GROUP = NH // NKV
INTER = 8192
WINDOW, SINK = 1024, 4
EPS = 1e-5
SCALE = 1.0 / math.sqrt(HS)
NTB = 16
OTB = 8
NCC = 16

KL = [list(range(12)), [0] + list(range(4, 16))]
BATCHES = [
    [(0, 4), (4, 4), (8, 4)],
    [(0, 4), (4, 4), (8, 4), (12, 1)],
]

ENG = ("sync", "tensor", "vector", "scalar", "gpsimd")


class Em:

    def __init__(self):
        self.items = {e: [] for e in ENG}
        self.cnt = {e: 0 for e in ENG}
        self.dma_cnt = {}
        self.keys = {}
        self.clock = {e: {} for e in ENG}
        self.hist = {e: [None] for e in ENG}
        self.streams = set()

    def task(self, eng, fns, reads=(), writes=(), war=(), stream=None):
        needed = {}
        for k in reads:
            ent = self.keys.get(k)
            if ent is not None and ent[0] is not None:
                s, c = ent[0], ent[1]
                if needed.get(s, 0) < c:
                    needed[s] = c
        for k in tuple(writes) + tuple(war):
            ent = self.keys.get(k)
            if ent is not None:
                if ent[0] is not None and needed.get(ent[0], 0) < ent[1]:
                    needed[ent[0]] = ent[1]
                for (s, c) in ent[2]:
                    if needed.get(s, 0) < c:
                        needed[s] = c
        clock = self.clock[eng]
        waits = []
        for s in sorted(needed):
            c = needed[s]
            if clock.get(s, 0) >= c:
                continue
            waits.append((s, c))
            snap = self.hist[s][c] if c < len(self.hist[s]) else None
            if snap:
                for s2, c2 in snap.items():
                    if clock.get(s2, 0) < c2:
                        clock[s2] = c2
            if clock.get(s, 0) < c:
                clock[s] = c
        if not fns:
            self.items[eng].append((waits, [], None))
            return
        if stream is None:
            self.cnt[eng] += 1
            mycnt = self.cnt[eng]
            src_id = eng
            inc = (eng, 1)
        else:
            self.streams.add(stream)
            self.dma_cnt[stream] = self.dma_cnt.get(stream, 0) + 1
            mycnt = self.dma_cnt[stream]
            src_id = stream
            inc = (stream, 16)
            if stream not in self.hist:
                self.hist[stream] = [None]
        snap = dict(clock)
        snap[src_id] = mycnt
        h = self.hist.setdefault(src_id, [None])
        assert len(h) == mycnt, (src_id, len(h), mycnt)
        h.append(snap)
        for k in reads:
            ent = self.keys.setdefault(k, [None, 0, []])
            ent[2].append((src_id, mycnt))
        for k in writes:
            self.keys[k] = [src_id, mycnt, []]
        self.items[eng].append((waits, fns, inc))

    def wait_only(self, eng, src, cnt):
        self.items[eng].append(([(src, cnt)], [], None))

    def replay(self, eng_obj, eng_name, sems):
        for waits, fns, inc in self.items[eng_name]:
            for (s, v) in waits:
                eng_obj.wait_ge(sems[s], 16 * v if s in self.streams else v)
            last = None
            for fn in fns:
                last = fn(eng_obj)
            if inc is not None and last is not None:
                last.then_inc(sems[inc[0]], inc[1])


def _ap(t, off, dims):
    return bass.AP(t, off, dims)


def build_nc():
    nc = bass.Bass("TRN2", target_bir_lowering=False)

    x_kv = nc.dram_tensor("x_kv", [2048, 2048], F32, kind="ExternalInput").ap()
    wq = nc.dram_tensor("wq_pe", [16, 128, 2048], F32, kind="ExternalInput").ap()
    wk = nc.dram_tensor("wk_pe", [4, 128, 2048], F32, kind="ExternalInput").ap()
    wv = nc.dram_tensor("wv_t", [2048, 512], F32, kind="ExternalInput").ap()
    wo = nc.dram_tensor("wo_t", [2048, 2048], F32, kind="ExternalInput").ap()
    wgu = nc.dram_tensor("wgu_pe", [128, 128, 2048], F32, kind="ExternalInput").ap()
    wd = nc.dram_tensor("wd_pe", [64, 128, 2048], F32, kind="ExternalInput").ap()
    mk = nc.dram_tensor("masks", [2, 128, 13 * 512], BF16, kind="ExternalInput").ap()
    id16 = nc.dram_tensor("ident16", [128, 128], BF16, kind="ExternalInput").ap()
    id32 = nc.dram_tensor("ident32", [128, 128], F32, kind="ExternalInput").ap()
    x2s = nc.dram_tensor("x2s", [1024, 2048], F32).ap()
    x_out = nc.dram_tensor("x_out", [1024, 2048], F32, kind="ExternalOutput").ap()

    _probe = nc.alloc_sbuf_tensor("base_probe", [128, 32], mybir.dt.uint8)
    BASE = nc.lookup_mloc(_probe).addr + 32
    nc.alloc_sbuf_tensor("manual_region", [128, 208672], mybir.dt.uint8)

    def A(name, shape, dtype, offset):
        return nc.alloc_sbuf_tensor_at(name, shape, dtype, offset=BASE + offset)
    h1f = A("h1f", [128, 32768], BF16, offset=0)
    wo_bf = A("wo_bf", [128, 32768], BF16, offset=0)
    out_fm = A("out_fm", [128, 16384], F32, offset=0)

    qf = A("qf", [128, 16384], BF16, offset=65536)
    h2f = A("h2f", [128, 16384], BF16, offset=65536)

    xt = [A("xt0", [128, 2048], F32, offset=98304),
          A("xt1", [128, 2048], F32, offset=106496)]
    h1t = [A("h1t0", [128, 2048], BF16, offset=114688),
           A("h1t1", [128, 2048], BF16, offset=118784)]
    wqr = [A(f"wqr{i}", [128, 2048], BF16, offset=98304 + 4096 * i) for i in range(4)]
    yf = A("yf", [128, 16384], BF16, offset=98304)
    a_sb = A("a_sb", [128, 16384], BF16, offset=98304)

    kf = A("kf", [128, 8192], BF16, offset=131072)
    vt = A("vt", [128, 8192], BF16, offset=147456)
    xo = [A("xo0", [128, 2048], F32, offset=131072),
          A("xo1", [128, 2048], F32, offset=139264)]
    x2t = A("x2t", [128, 2048], F32, offset=147456)
    outT = A("outT", [128, 2048], F32, offset=155648)

    wkbf = A("wkbf", [128, 8192], BF16, offset=163840)
    wvbf = A("wvbf", [128, 8192], BF16, offset=180224)
    msk = A("msk", [128, 6656], BF16, offset=163840)
    pt = A("pt", [128, 6656], BF16, offset=177152)
    zb_sb = A("zb_sb", [128, 512], F32, offset=190464)
    zb2_sb = A("zb2_sb", [128, 512], F32, offset=192512)
    zrow2 = A("zrow2", [1, 512], BF16, offset=194560)
    wbf = ([A(f"wbf{i}", [128, 2048], BF16, offset=163840 + 4096 * i) for i in range(4)]
           + [A("wbf4", [128, 2048], BF16, offset=184320)])
    silu_t = A("silu_t", [128, 1024], F32, offset=180224)

    wst = [A("wst0", [128, 1024], F32, offset=196608),
           A("wst1", [128, 1024], F32, offset=200704)]

    ident16 = A("i16", [128, 128], BF16, offset=204800)
    ident32 = A("i32", [128, 128], F32, offset=205056)
    ones_bf = A("ones_bf", [128, 1], BF16, offset=205568)
    ones_row = A("ones_row", [1, 128], BF16, offset=205600)
    eps_sb = A("eps_sb", [128, 1], F32, offset=206112)
    bn6 = A("bn6", [128, 96], F32, offset=206144)
    bn2 = A("bn2", [128, 2], F32, offset=206528)
    rstd = A("rstd", [128, 1], F32, offset=206560)
    nmr = A("nmr", [128, 1], F32, offset=206592)
    zinv = A("zinv", [1, 512], BF16, offset=206624)
    bn6b = A("bn6b", [128, 96], F32, offset=208672)
    bn2b = A("bn2b", [128, 2], F32, offset=209056)
    rstdb = A("rstdb", [128, 1], F32, offset=209088)
    nmrb = A("nmrb", [128, 1], F32, offset=209120)
    h2tok = A("h2tok", [128, 2048], BF16, offset=180224)

    ps_a = nc.alloc_psum_tensor("ps_a", [128, 2048], F32)
    ps_b2 = nc.alloc_psum_tensor("ps_b2", [128, 2048], F32)
    pa = ps_a.ap()
    pb2 = ps_b2.ap()
    pb = pb2[:, 0:1024]
    pc = pb2[:, 1024:1536]
    pz = pb2[:, 1536:2048]
    pa_bf = pa.bitcast(BF16)
    b2_bf = pb2.bitcast(BF16)
    pb_bf = b2_bf[:, 0:2048]
    pc_bf = b2_bf[:, 2048:3072]
    pz_bf = b2_bf[:, 3072:4096]

    PA = [pa[:, 512 * i:512 * (i + 1)] for i in range(4)]
    PB = [pb[:, 512 * i:512 * (i + 1)] for i in range(2)]
    PAK = [f"pA{i}" for i in range(4)]
    PBK = [f"pB{i}" for i in range(2)]

    em = Em()
    WK = [f"wk{m}" for m in range(4)]
    WV = [f"wv{c}" for c in range(16)]
    WQR = [f"wqr{i}" for i in range(4)]
    WO = [f"wo{c}_{h}" for c in range(16) for h in range(2)]
    H1K = [f"h1_{tb}_{g}" for tb in range(16) for g in range(2)]
    QK_ = [f"q_{h}_{s}" for h in range(16) for s in range(2)]
    KK = [f"k_{m}_{t}" for m in range(4) for t in range(4)]
    VK = [f"v_{t}" for t in range(16)]
    YK = [f"y_{h}_{s}" for h in range(16) for s in range(2)]
    PTK = [f"pt{s}" for s in range(13)]
    H2K = [f"h2_{tb}" for tb in range(8)]
    AK = [f"a_{j}" for j in range(16)]
    OFK = [f"of_{m}" for m in range(16)]
    WBFK = [f"wbf{i}" for i in range(4)]

    em.task("gpsimd", [lambda g: g.memset(eps_sb.ap(), EPS)], writes=["eps"])
    em.task("gpsimd", [lambda g: g.memset(ones_bf.ap(), 1.0)], writes=["ones_bf"])
    em.task("gpsimd", [lambda g: g.memset(ones_row.ap(), 1.0)], writes=["ones_row"])
    em.task("sync", [lambda s: s.dma_start(out=ident16.ap(), in_=id16[:, :])],
            writes=["i16"], stream="cst16")
    em.task("sync", [lambda s: s.dma_start(out=ident32.ap(), in_=id32[:, :])],
            writes=["i32"], stream="cst32")

    st_ct = [0]

    def stage(src_ap, dst_ap, dst_keys, war=(), n=1024, eng="gpsimd"):
        s = st_ct[0] % 2
        st_ct[0] += 1
        slot = wst[s]
        em.task("sync",
                [lambda q, src=src_ap, d=slot, n=n: q.dma_start(out=d.ap()[:, 0:n], in_=src)],
                writes=[f"wst{s}"], stream=f"wst{s}")
        if eng == "scalar":
            fn = [lambda g, d=dst_ap, sl=slot, n=n: g.copy(d, sl.ap()[:, 0:n])]
        else:
            fn = [lambda g, d=dst_ap, sl=slot, n=n: g.tensor_copy(d, sl.ap()[:, 0:n])]
        em.task(eng, fn, reads=[f"wst{s}"], writes=list(dst_keys), war=list(war))

    wkv_stages = []
    for m in range(4):
        for h in range(2):
            wkv_stages.append(lambda m=m, h=h: stage(
                wk[m, :, 1024 * h:1024 * (h + 1)],
                wkbf.ap()[:, m * 2048 + 1024 * h: m * 2048 + 1024 * (h + 1)],
                [f"wk{m}"] if h == 1 else [f"wk{m}_half"], eng="scalar"))
    for c in range(16):
        wkv_stages.append(lambda c=c: stage(
            wv[c * 128:(c + 1) * 128, 0:512],
            wvbf.ap()[:, c * 512:(c + 1) * 512], [f"wv{c}"], n=512))

    STT = [(bn6, bn2, rstd, nmr, ""), (bn6b, bn2b, rstdb, nmrb, "b")]

    def ln_stats(tile_aps, tile_keys, n_chunks, par, emit_nmr=False):
        b6, b2, rs, nm, sx = STT[par]
        fns = []
        ci = [0]
        for ap_, w in tile_aps:
            for j in range(w // 512):
                i = ci[0]
                fns.append(lambda v, a=ap_, j=j, i=i, b6=b6: v.bn_stats(
                    b6.ap()[:, 6 * i:6 * i + 6], a[:, 512 * j:512 * (j + 1)]))
                ci[0] += 1
        assert ci[0] == n_chunks
        em.task("vector", fns, reads=list(tile_keys), writes=["bn6" + sx])
        em.task("vector", [lambda v, b6=b6, b2=b2, n=n_chunks: v.bn_aggr(
            b2.ap(), b6.ap()[:, 0:6 * n])],
                reads=["bn6" + sx], writes=["bn2" + sx])
        em.task("scalar", [lambda s, b2=b2, rs=rs: s.activation(
            rs.ap(), b2.ap()[:, 1:2], AF.Ln, bias=eps_sb.ap(), scale=1.0)],
                reads=["bn2" + sx, "eps"], writes=["rstd" + sx])
        em.task("scalar", [lambda s, rs=rs: s.activation(
            rs.ap(), rs.ap(), AF.Exp, scale=-0.5)],
                reads=["rstd" + sx], writes=["rstd" + sx])
        if emit_nmr:
            em.task("vector", [lambda v, b2=b2, rs=rs, nm=nm: v.tensor_scalar(
                nm.ap(), b2.ap()[:, 0:1], rs.ap(), -1.0, ALU.mult, ALU.mult)],
                    reads=["bn2" + sx, "rstd" + sx], writes=["nmr" + sx])


    rot = [0]

    def bank512():
        i = rot[0] % 6
        rot[0] += 1
        if i < 4:
            return PA[i], PAK[i]
        return PB[i - 4], PBK[i - 4]

    def kproj(m, ts):
        bk, bkk = bank512()
        fns = []
        for c in range(16):
            fns.append(lambda t, m=m, ts=ts, c=c, bk=bk: t.matmul(
                bk, wkbf.ap()[:, m * 2048 + c * 128:m * 2048 + (c + 1) * 128],
                h1f.ap()[:, c * 2048 + ts * 512:c * 2048 + (ts + 1) * 512],
                start=(c == 0), stop=(c == 15), skip_group_check=True))
        rd = [f"wk{m}"] + [f"h1_{4 * ts + i}_{g}" for i in range(4) for g in range(2)]
        em.task("tensor", fns, reads=rd, writes=[bkk])
        em.task("scalar",
                [lambda q, m=m, ts=ts, bk=bk: q.copy(
                    kf.ap()[:, m * 2048 + ts * 512:m * 2048 + (ts + 1) * 512], bk)],
                reads=[bkk], writes=[f"k_{m}_{ts}"])

    def vproj(tb):
        bk, bkk = bank512()
        fns = []
        for c in range(16):
            fns.append(lambda t, tb=tb, c=c, bk=bk: t.matmul(
                bk, h1f.ap()[:, c * 2048 + tb * 128:c * 2048 + (tb + 1) * 128],
                wvbf.ap()[:, c * 512:(c + 1) * 512],
                start=(c == 0), stop=(c == 15), skip_group_check=True))
        rd = [f"wv{c}" for c in range(16)] + [f"h1_{tb}_{g}" for g in range(2)]
        em.task("tensor", fns, reads=rd, writes=[bkk])
        em.task("scalar",
                [lambda q, tb=tb, bk=bk: q.copy(
                    vt.ap()[:, tb * 512:(tb + 1) * 512], bk)],
                reads=[bkk], writes=[f"v_{tb}"])

    def stage_wq(me):
        r = me % 4
        war = ["xt0", "xt1", "h1t0", "h1t1"] if me < 4 else []
        for h in range(2):
            stage(wq[me, :, 1024 * h:1024 * (h + 1)],
                  wqr[r].ap()[:, 1024 * h:1024 * (h + 1)],
                  [f"wqr{r}"] if h == 1 else [f"wqr{r}_h"],
                  war=war + ([f"wqr{r}"] if h == 0 else []), eng="scalar")

    TB_ORDER = list(range(8, 16)) + list(range(8))
    vback = []
    for idx, tb in enumerate(TB_ORDER):
        s = idx % 2
        par = idx % 2
        sx = STT[par][4]
        if idx >= 6:
            while len(vback) > 1:
                vproj(vback.pop(0))
        if idx == 4:
            for m in range(4):
                kproj(m, 2)
        elif idx == 8:
            for m in range(4):
                kproj(m, 3)
        elif idx == 12:
            for m in range(4):
                kproj(m, 0)
        em.task("gpsimd",
                [lambda q, tb=tb, s=s: q.dma_start(out=xt[s].ap(),
                                                   in_=x_kv[tb * 128:(tb + 1) * 128, :])],
                writes=[f"xt{s}"], stream=f"xt{s}")
        for _ in range(4):
            if wkv_stages:
                wkv_stages.pop(0)()
        ln_stats([(xt[s].ap(), 2048)], [f"xt{s}"], 4, par, emit_nmr=True)
        em.task("scalar",
                [lambda q, s=s, par=par: q.activation(
                    h1t[s].ap(), xt[s].ap(), AF.Identity,
                    bias=STT[par][3].ap(), scale=STT[par][2].ap())],
                reads=[f"xt{s}", "nmr" + sx, "rstd" + sx], writes=[f"h1t{s}"])
        for g in range(2):
            tbf, tbk = (pc_bf, "pC") if g == 0 else (pz_bf, "pZ")
            fns = []
            for c8 in range(8):
                fns.append(lambda t, s=s, g=g, c8=c8, tbf=tbf: t.transpose(
                    tbf[:, c8 * 128:(c8 + 1) * 128],
                    h1t[s].ap()[:, (g * 8 + c8) * 128:(g * 8 + c8 + 1) * 128],
                    ident16.ap()))
            em.task("tensor", fns, reads=[f"h1t{s}", "i16"], writes=[tbk])
        dst = _ap(h1f, tb * 128, [[32768, 128], [2048, 16], [1, 128]])
        em.task("scalar",
                [lambda q, dst=dst: q.copy(dst, b2_bf[:, 2048:4096])],
                reads=["pC", "pZ"], writes=[f"h1_{tb}_0", f"h1_{tb}_1"])
        vback.append(tb)

    while vback:
        vproj(vback.pop(0))
    for m in range(4):
        kproj(m, 1)

    stage_wq(0)
    stage_wq(1)
    for me in range(16):
        r = me % 4
        for ts in range(2):
            bk, bkk = bank512()
            fns = []
            for c in range(16):
                fns.append(lambda t, r=r, ts=ts, c=c, bk=bk: t.matmul(
                    bk, wqr[r].ap()[:, c * 128:(c + 1) * 128],
                    h1f.ap()[:, c * 2048 + 1024 + ts * 512:c * 2048 + 1024 + (ts + 1) * 512],
                    start=(c == 0), stop=(c == 15), skip_group_check=True))
            rd = [f"wqr{r}"] + [f"h1_{8 + 4 * ts + i}_{g}" for i in range(4) for g in range(2)]
            em.task("tensor", fns, reads=rd, writes=[bkk])
            em.task("scalar",
                    [lambda q, me=me, ts=ts, bk=bk: q.copy(
                        qf.ap()[:, me * 1024 + ts * 512:me * 1024 + (ts + 1) * 512], bk)],
                    reads=[bkk], writes=[f"q_{me}_{ts}"])
        if me + 2 < 16:
            stage_wq(me + 2)

    em.task("sync", [lambda s: s.dma_start(out=msk.ap(), in_=mk[0, :, :])],
            writes=["masks"], war=WK + [f"wk{m}_half" for m in range(4)], stream="msk")

    for c in range(16):
        for h in range(2):
            stage(wo[c * 128:(c + 1) * 128, 1024 * h:1024 * (h + 1)],
                  wo_bf.ap()[:, c * 2048 + 1024 * h:c * 2048 + 1024 * (h + 1)],
                  [f"wo{c}_{h}"],
                  war=H1K if (c == 0 and h == 0) else [])

    first_exp = [True]
    YB = [(pc, "pC"), (PB[0], "pB0")]
    ZB = [(pz, "pZ"), (PB[1], "pB1")]
    ZROW = [zinv, zrow2]
    ZBB = [zb_sb, zb2_sb]

    def front_batch(qs, hh, bi):
        kl = KL[qs]
        kv = hh // 4
        st0, ln = BATCHES[qs][bi]
        fns = []
        bkeys = []
        for j in range(ln):
            s_ = st0 + j
            kb = kl[s_]
            bkeys.append(PAK[j])
            fns.append(lambda t, kb=kb, kv=kv, dst=PA[j], qs=qs, hh=hh:
                       t.matmul(dst,
                                kf.ap()[:, kv * 2048 + kb * 128:kv * 2048 + (kb + 1) * 128],
                                qf.ap()[:, hh * 1024 + qs * 512:hh * 1024 + (qs + 1) * 512],
                                start=True, stop=True, skip_group_check=True))
        rd = [f"k_{kv}_{kl[st0 + j] // 4}" for j in range(ln)] + [f"q_{hh}_{qs}"]
        em.task("tensor", fns, reads=rd, writes=bkeys)
        war = (WK + WV + [f"wk{m}_half" for m in range(4)] if first_exp[0] else [])
        first_exp[0] = False
        em.task("scalar",
                [lambda q, src=pa[:, 0:512 * ln], st0=st0, ln=ln: q.activation(
                    pt.ap()[:, st0 * 512:(st0 + ln) * 512], src,
                    AF.Exp, scale=SCALE)],
                reads=bkeys, writes=[f"pt{st0 + j}" for j in range(ln)],
                war=war)
        full = set(range(5, 9)) if qs == 1 else set()
        segs = []
        j = 0
        while j < ln:
            if (st0 + j) in full:
                j += 1
                continue
            j0 = j
            while j < ln and (st0 + j) not in full:
                j += 1
            segs.append((st0 + j0, j - j0))
        if segs:
            em.task("vector",
                    [lambda v, a=a, b=b: v.tensor_tensor(
                        pt.ap()[:, a * 512:(a + b) * 512],
                        pt.ap()[:, a * 512:(a + b) * 512],
                        msk.ap()[:, a * 512:(a + b) * 512], ALU.mult)
                     for (a, b) in segs],
                    reads=["masks"] + [f"pt{st0 + j}" for j in range(ln)],
                    writes=[f"pt{st0 + j}" for j in range(ln)])

    def back_batch(qs, hh, par, bi):
        kl = KL[qs]
        kv = hh // 4
        nk = len(kl)
        st0, ln = BATCHES[qs][bi]
        yb, ybk = YB[par]
        zb, zbk = ZB[par]
        fns = []
        for j in range(ln):
            s_ = st0 + j
            kb = kl[s_]
            first = (s_ == 0)
            last = (s_ == nk - 1)
            fns.append(lambda t, kb=kb, s_=s_, kv=kv, first=first, last=last, yb=yb:
                       t.matmul(yb,
                                vt.ap()[:, kb * 512 + kv * 128:kb * 512 + (kv + 1) * 128],
                                pt.ap()[:, s_ * 512:(s_ + 1) * 512],
                                start=first, stop=last, skip_group_check=True))
        for j in range(ln):
            s_ = st0 + j
            first = (s_ == 0)
            last = (s_ == nk - 1)
            fns.append(lambda t, s_=s_, first=first, last=last, zb=zb:
                       t.matmul(zb[0:1, :], ones_bf.ap(),
                                pt.ap()[:, s_ * 512:(s_ + 1) * 512],
                                start=first, stop=last, skip_group_check=True))
        rd = [f"pt{st0 + j}" for j in range(ln)]
        rd += [f"v_{kl[st0 + j]}" for j in range(ln)] + ["ones_bf", ybk, zbk]
        em.task("tensor", fns, reads=rd, writes=[ybk, zbk])

    def back_norm_pre(qs, hh, par):
        zb, zbk = ZB[par]
        zr = ZROW[par]
        em.task("scalar", [lambda q, zr=zr, zb=zb: q.copy(zr.ap(), zb[0:1, :])],
                reads=[zbk], writes=[f"zrow{par}"])

    def back_norm(qs, hh, par):
        yb, ybk = YB[par]
        zb, zbk = ZB[par]
        zr = ZROW[par]
        zbb = ZBB[par]
        em.task("tensor",
                [lambda t, zb=zb, zr=zr: t.matmul(zb[:, :], ones_row.ap(), zr.ap(),
                                                  start=True, stop=True,
                                                  skip_group_check=True)],
                reads=[f"zrow{par}", "ones_row", zbk], writes=[zbk])
        em.task("scalar", [lambda q, zbb=zbb, zb=zb: q.copy(zbb.ap(), zb[:, :])],
                reads=[zbk], writes=[f"zb{par}"])
        em.task("scalar", [lambda q, zbb=zbb: q.activation(
            zbb.ap(), zbb.ap(), AF.Ln, scale=1.0)],
                reads=[f"zb{par}"], writes=[f"zb{par}"])
        em.task("scalar", [lambda q, zbb=zbb: q.activation(
            zbb.ap(), zbb.ap(), AF.Exp, scale=-1.0)],
                reads=[f"zb{par}"], writes=[f"zb{par}"])
        war = WQR if (qs == 0 and hh == 0) else []
        em.task("vector",
                [lambda v, hh=hh, qs=qs, yb=yb, zbb=zbb: v.tensor_tensor(
                    yf.ap()[:, hh * 1024 + qs * 512:hh * 1024 + (qs + 1) * 512],
                    yb[:, :], zbb.ap(), ALU.mult)],
                reads=[ybk, f"zb{par}"], writes=[f"y_{hh}_{qs}"], war=war)

    HSEQ = [(qs, hh) for qs in range(2) for hh in range(16)]
    for i, (qs, hh) in enumerate(HSEQ):
        if qs == 1 and hh == 0:
            em.task("sync", [lambda s: s.dma_start(out=msk.ap(), in_=mk[1, :, :])],
                    writes=["masks"], stream="msk")
        nbf = len(BATCHES[qs])
        nbb = len(BATCHES[HSEQ[i - 1][0]]) if i >= 1 else 0
        for bi in range(max(nbf, nbb)):
            if i >= 1 and bi < nbb:
                back_batch(HSEQ[i - 1][0], HSEQ[i - 1][1], (i - 1) % 2, bi)
            if bi < nbf:
                front_batch(qs, hh, bi)
            if bi == 0 and i >= 2:
                back_norm(HSEQ[i - 2][0], HSEQ[i - 2][1], (i - 2) % 2)
        if i >= 1:
            back_norm_pre(HSEQ[i - 1][0], HSEQ[i - 1][1], (i - 1) % 2)
    n = len(HSEQ)
    back_norm(HSEQ[n - 2][0], HSEQ[n - 2][1], (n - 2) % 2)
    for bi in range(len(BATCHES[HSEQ[n - 1][0]])):
        back_batch(HSEQ[n - 1][0], HSEQ[n - 1][1], (n - 1) % 2, bi)
    back_norm_pre(HSEQ[n - 1][0], HSEQ[n - 1][1], (n - 1) % 2)
    back_norm(HSEQ[n - 1][0], HSEQ[n - 1][1], (n - 1) % 2)

    pend_c = []
    for tb in range(8):
        qs = tb // 4
        s = tb % 2
        par = tb % 2
        sx = STT[par][4]
        x2b = x2t if s == 0 else outT
        xk = f"x2t_{s}"
        em.task("gpsimd",
                [lambda q, tb=tb, s=s: q.dma_start(
                    out=xo[s].ap(), in_=x_kv[(8 + tb) * 128:(9 + tb) * 128, :])],
                writes=[f"xo{s}"], war=(KK if tb < 2 else []), stream=f"xo{s}")
        if tb % 2 == 0:
            banks = [(PA[i], PAK[i]) for i in range(4)]
            tp_bf = pa_bf
            tpk = ["pA0", "pA1"]
        else:
            banks = [(PB[0], PBK[0]), (PB[1], PBK[1]), (pc, "pC"), (pz, "pZ")]
            tp_bf = pb_bf
            tpk = ["pB0", "pB1"]
        fns = []
        for c in range(16):
            for ms in range(4):
                fns.append(lambda t, c=c, ms=ms, tb=tb, bk=banks[ms][0]: t.matmul(
                    bk, yf.ap()[:, c * 1024 + tb * 128:c * 1024 + (tb + 1) * 128],
                    wo_bf.ap()[:, c * 2048 + ms * 512:c * 2048 + (ms + 1) * 512],
                    start=(c == 0), stop=(c == 15), skip_group_check=True))
        rd = [f"y_{c}_{qs}" for c in range(16)] + WO
        em.task("tensor", fns, reads=rd, writes=[bk for _, bk in banks])
        if pend_c:
            for f in pend_c.pop(0):
                f()
        ln_stats([(bk, 512) for bk, _ in banks], [bk for _, bk in banks], 4, par,
                 emit_nmr=True)
        whole = pa if tb % 2 == 0 else pb2
        em.task("scalar",
                [lambda q, whole=whole, par=par, x2b=x2b: q.activation(
                    x2b.ap(), whole, AF.Identity,
                    bias=STT[par][3].ap(), scale=STT[par][2].ap())],
                reads=[bk for _, bk in banks] + ["nmr" + sx, "rstd" + sx],
                writes=[f"x2p_{s}"],
                war=[xk] + (VK if tb < 2 else []))
        em.task("vector",
                [lambda v, s=s, x2b=x2b: v.tensor_tensor(x2b.ap(), x2b.ap(),
                                                         xo[s].ap(), ALU.add)],
                reads=[f"xo{s}", f"x2p_{s}"], writes=[xk])
        em.task("gpsimd",
                [lambda q, tb=tb, x2b=x2b: q.dma_start(
                    out=x2s[tb * 128:(tb + 1) * 128, :], in_=x2b.ap())],
                reads=[xk], writes=[f"x2s_{tb}"], stream=f"x2w{s}")
        ln_stats([(x2b.ap(), 2048)], [xk], 4, par, emit_nmr=True)
        em.task("scalar",
                [lambda q, par=par, x2b=x2b: q.activation(
                    h2tok.ap(), x2b.ap(), AF.Identity,
                    bias=STT[par][3].ap(), scale=STT[par][2].ap())],
                reads=[xk, "nmr" + sx, "rstd" + sx], writes=["h2tok"],
                war=((VK + PTK + WV) if tb == 0 else []))

        def emit_tp(tb=tb, tp_bf=tp_bf, tpk=tpk):
            fns = []
            for c in range(16):
                fns.append(lambda t, c=c, tp_bf=tp_bf: t.transpose(
                    tp_bf[:, c * 128:(c + 1) * 128],
                    h2tok.ap()[:, c * 128:(c + 1) * 128], ident16.ap()))
            em.task("tensor", fns, reads=["h2tok", "i16"], writes=tpk)

        def emit_ev(tb=tb, tp_bf=tp_bf, tpk=tpk):
            dst = _ap(h2f, tb * 128, [[16384, 128], [1024, 16], [1, 128]])
            em.task("scalar",
                    [lambda q, dst=dst, tp_bf=tp_bf: q.copy(dst, tp_bf[:, 0:2048])],
                    reads=tpk, writes=[f"h2_{tb}"],
                    war=(QK_ if tb == 0 else []))
        pend_c.append([emit_tp, emit_ev])
    while pend_c:
        for f in pend_c.pop(0):
            f()

    went = [0]
    wbf_used = set()

    def wslot():
        r = went[0] % 5
        went[0] += 1
        return r

    for q in range(4):
        for jj in range(16):
            j = q * 16 + jj
            rg = wslot()
            ru = wslot()
            war_g = (PTK + ["masks"] if rg not in wbf_used else [])
            wbf_used.add(rg)
            for h in range(2):
                stage(wgu[2 * j, :, 1024 * h:1024 * (h + 1)],
                      wbf[rg].ap()[:, 1024 * h:1024 * (h + 1)],
                      [f"wbf{rg}"] if h == 1 else [f"wbf{rg}_h"],
                      war=war_g + ([f"wbf{rg}"] if h == 0 else []), eng="scalar")
            war_u = (PTK + ["masks"] if ru not in wbf_used else [])
            wbf_used.add(ru)
            for h in range(2):
                stage(wgu[2 * j + 1, :, 1024 * h:1024 * (h + 1)],
                      wbf[ru].ap()[:, 1024 * h:1024 * (h + 1)],
                      [f"wbf{ru}"] if h == 1 else [f"wbf{ru}_h"],
                      war=war_u + ([f"wbf{ru}"] if h == 0 else []), eng="scalar")
            fns = []
            for c in range(16):
                for ts in range(2):
                    fns.append(lambda t, c=c, ts=ts, rg=rg: t.matmul(
                        PB[ts], wbf[rg].ap()[:, c * 128:(c + 1) * 128],
                        h2f.ap()[:, c * 1024 + ts * 512:c * 1024 + (ts + 1) * 512],
                        start=(c == 0), stop=(c == 15), skip_group_check=True))
            em.task("tensor", fns, reads=[f"wbf{rg}"] + H2K, writes=PBK)
            fns = []
            for c in range(16):
                for ts in range(2):
                    fns.append(lambda t, c=c, ts=ts, ru=ru: t.matmul(
                        pa[:, ts * 512:(ts + 1) * 512],
                        wbf[ru].ap()[:, c * 128:(c + 1) * 128],
                        h2f.ap()[:, c * 1024 + ts * 512:c * 1024 + (ts + 1) * 512],
                        start=(c == 0), stop=(c == 15), skip_group_check=True))
            em.task("tensor", fns, reads=[f"wbf{ru}"] + H2K,
                    writes=["pA0", "pA1"])
            em.task("scalar",
                    [lambda q_: q_.activation(silu_t.ap(), pb[:, :], AF.Silu)],
                    reads=PBK, writes=["silu"],
                    war=((PTK + ["h2tok"]) if (q == 0 and jj == 0) else []))
            em.task("vector",
                    [lambda v, jj=jj: v.tensor_tensor(
                        a_sb.ap()[:, jj * 1024:(jj + 1) * 1024],
                        silu_t.ap(), pa[:, 0:1024], ALU.mult)],
                    reads=["silu", "pA0", "pA1"], writes=[f"a_{jj}"],
                    war=(YK if (q == 0 and jj == 0) else []))
        for mc in range(16):
            r = wslot()
            for h in range(2):
                stage(wd[q * 16 + mc, :, 1024 * h:1024 * (h + 1)],
                      wbf[r].ap()[:, 1024 * h:1024 * (h + 1)],
                      [f"wbf{r}"] if h == 1 else [f"wbf{r}_h"],
                      war=([f"wbf{r}"] if h == 0 else []), eng="scalar")
            if mc % 2 == 0:
                dps = [pa[:, 1024:1536], pa[:, 1536:2048]]
                dpk = ["pA2", "pA3"]
            else:
                dps = [pz, pc]
                dpk = ["pZ", "pC"]
            fns = []
            for ic in range(16):
                for ts in range(2):
                    fns.append(lambda t, ic=ic, ts=ts, r=r, dps=dps: t.matmul(
                        dps[ts], wbf[r].ap()[:, ic * 128:(ic + 1) * 128],
                        a_sb.ap()[:, ic * 1024 + ts * 512:ic * 1024 + (ts + 1) * 512],
                        start=(ic == 0), stop=(ic == 15), skip_group_check=True))
            em.task("tensor", fns, reads=[f"wbf{r}"] + AK, writes=dpk)
            if q == 0:
                war = WO if mc == 0 else []
                em.task("vector",
                        [lambda s_, mc=mc, dps=dps: s_.tensor_copy(
                            out_fm.ap()[:, mc * 1024:mc * 1024 + 512], dps[0]),
                         lambda s_, mc=mc, dps=dps: s_.tensor_copy(
                            out_fm.ap()[:, mc * 1024 + 512:(mc + 1) * 1024], dps[1])],
                        reads=dpk, writes=[f"of_{mc}"], war=war)
            else:
                em.task("vector",
                        [lambda v, mc=mc, dps=dps: v.tensor_tensor(
                            out_fm.ap()[:, mc * 1024:mc * 1024 + 512],
                            out_fm.ap()[:, mc * 1024:mc * 1024 + 512],
                            dps[0], ALU.add),
                         lambda v, mc=mc, dps=dps: v.tensor_tensor(
                            out_fm.ap()[:, mc * 1024 + 512:(mc + 1) * 1024],
                            out_fm.ap()[:, mc * 1024 + 512:(mc + 1) * 1024],
                            dps[1], ALU.add)],
                        reads=dpk + [f"of_{mc}"], writes=[f"of_{mc}"])

    em.task("sync", [], reads=[f"x2s_{t}" for t in range(8)])

    def e_stage1(tb):
        s = tb % 2
        par = tb % 2
        sx = STT[par][4]
        if s == 1:
            TPP = [(pb2[:, 0:512], "pB0"), (pb2[:, 512:1024], "pB1"),
                   (pb2[:, 1024:1536], "pC"), (pb2[:, 1536:2048], "pZ")]
            tsrc, tkeys = pb2, ["pB0", "pB1", "pC", "pZ"]
        else:
            TPP = [(pa[:, 512 * g:512 * (g + 1)], PAK[g]) for g in range(4)]
            tsrc, tkeys = pa, list(PAK)
        x2buf = x2t if s == 0 else xo[0]
        x2key = "x2t" if s == 0 else "xo0"
        otbuf = outT if s == 0 else xo[1]
        otkey = "outT" if s == 0 else "xo1"
        war_r = [x2key + "fin"]
        if s == 0:
            war_r += ["x2t_0", "x2p_0", "h2tok"]
        em.task("sync",
                [lambda q, tb=tb, x2buf=x2buf: q.dma_start(
                    out=x2buf.ap(), in_=x2s[tb * 128:(tb + 1) * 128, :])],
                reads=[f"x2s_{tb}"], writes=[x2key], war=war_r, stream=f"x2r{s}")
        for g in range(4):
            tpp, tppk = TPP[g]
            fns = []
            for k in range(4):
                mc = g * 4 + k
                fns.append(lambda t, mc=mc, k=k, tb=tb, tpp=tpp: t.transpose(
                    tpp[:, k * 128:(k + 1) * 128],
                    out_fm.ap()[:, mc * 1024 + tb * 128:mc * 1024 + (tb + 1) * 128],
                    ident32.ap()))
            em.task("tensor", fns,
                    reads=[f"of_{g * 4 + k}" for k in range(4)] + ["i32"],
                    writes=[tppk])
        b6, b2_, rs, nm, _ = STT[par]
        fns = []
        for j in range(4):
            fns.append(lambda v, j=j, b6=b6, tsrc=tsrc: v.bn_stats(
                b6.ap()[:, 6 * j:6 * j + 6],
                tsrc[:, 512 * j:512 * (j + 1)]))
        em.task("vector", fns, reads=tkeys, writes=["bn6" + sx])
        em.task("vector", [lambda v, b6=b6, b2_=b2_: v.bn_aggr(
            b2_.ap(), b6.ap()[:, 0:24])],
                reads=["bn6" + sx], writes=["bn2" + sx])

    def e_stage2(tb):
        s = tb % 2
        par = tb % 2
        sx = STT[par][4]
        x2buf = x2t if s == 0 else xo[0]
        x2key = "x2t" if s == 0 else "xo0"
        otbuf = outT if s == 0 else xo[1]
        otkey = "outT" if s == 0 else "xo1"
        if s == 1:
            tsrc, tkeys = pb2, ["pB0", "pB1", "pC", "pZ"]
        else:
            tsrc, tkeys = pa, list(PAK)
        b6, b2_, rs, nm, _ = STT[par]
        em.task("scalar", [lambda q, b2_=b2_, rs=rs: q.activation(
            rs.ap(), b2_.ap()[:, 1:2], AF.Ln, bias=eps_sb.ap(), scale=1.0)],
                reads=["bn2" + sx, "eps"], writes=["rstd" + sx])
        em.task("scalar", [lambda q, rs=rs: q.activation(
            rs.ap(), rs.ap(), AF.Exp, scale=-0.5)],
                reads=["rstd" + sx], writes=["rstd" + sx])
        em.task("vector", [lambda v, b2_=b2_, rs=rs, nm=nm: v.tensor_scalar(
            nm.ap(), b2_.ap()[:, 0:1], rs.ap(), -1.0, ALU.mult, ALU.mult)],
                reads=["bn2" + sx, "rstd" + sx], writes=["nmr" + sx])
        war_o = [otkey + "ln"] + (["xo1"] if s == 1 else
                                  ["h2tok", "x2t_1", "x2p_1"])
        em.task("scalar",
                [lambda q, otbuf=otbuf, par=par, tsrc=tsrc: q.activation(
                    otbuf.ap(), tsrc, AF.Identity,
                    bias=STT[par][3].ap(), scale=STT[par][2].ap())],
                reads=tkeys + ["nmr" + sx, "rstd" + sx],
                writes=[otkey + "ln"], war=war_o)
        em.task("gpsimd",
                [lambda v, x2buf=x2buf, otbuf=otbuf: v.tensor_tensor(
                    x2buf.ap(), x2buf.ap(), otbuf.ap(), ALU.add)],
                reads=[x2key, otkey + "ln"], writes=[x2key + "fin"])
        em.task("sync",
                [lambda q, tb=tb, x2buf=x2buf: q.dma_start(
                    out=x_out[tb * 128:(tb + 1) * 128, :], in_=x2buf.ap())],
                reads=[x2key + "fin"], writes=[x2key], stream=f"out{tb}")

    for tb in range(8):
        e_stage1(tb)
        if tb >= 1:
            e_stage2(tb - 1)
    e_stage2(7)

    for tb in range(8):
        em.wait_only("scalar", f"out{tb}", 1)
        em.wait_only("sync", f"out{tb}", 1)

    sems = {}
    sem_names = list(ENG) + sorted(em.streams)
    for s in sem_names:
        sems[s] = nc.alloc_semaphore(name=f"s_{s}")

    with nc.Block() as block:
        @block.sync
        def _(e):
            em.replay(e, "sync", sems)

        @block.tensor
        def _(e):
            em.replay(e, "tensor", sems)

        @block.vector
        def _(e):
            em.replay(e, "vector", sems)

        @block.scalar
        def _(e):
            em.replay(e, "scalar", sems)

        @block.gpsimd
        def _(e):
            em.replay(e, "gpsimd", sems)

    return nc



def _build_mask_tiles():
    T = 2048
    i = np.arange(T)[:, None]
    j = np.arange(T)[None, :]
    causal = j <= i
    band = j >= i - (WINDOW - 1)
    m = causal & band
    m = m | (j < SINK)
    m = m & ((i >= SINK) | causal)
    out = np.zeros((2, 2, 128, 13 * 512), dtype=BF)
    for par in range(2):
        off = 1024 if par == 0 else 0
        F = np.zeros((2048, 2048), dtype=bool)
        if par == 0:
            F[1024:, 1024:] = m[:1024, :1024]
        else:
            F[1024:, :] = m[1024:, :]
        for qs in range(2):
            q0 = 1024 + qs * 512
            for s_, kb in enumerate(KL[qs]):
                tile = F[q0:q0 + 512, kb * 128:(kb + 1) * 128].T
                out[par, qs, :, s_ * 512:(s_ + 1) * 512] = tile.astype(BF)
    return out


def _prep_weights(Wq, Wk, Wv, Wo, Wgu, Wd):
    wq_pe = np.ascontiguousarray(
        Wq.reshape(16, 128, 16, 128).transpose(0, 3, 2, 1)).reshape(16, 128, 2048)
    wk_pe = np.ascontiguousarray(
        Wk.reshape(4, 128, 16, 128).transpose(0, 3, 2, 1)).reshape(4, 128, 2048)
    wv_t = np.ascontiguousarray(Wv.T)
    wo_t = np.ascontiguousarray(Wo.T)
    gu = np.ascontiguousarray(
        Wgu.reshape(128, 128, 16, 128).transpose(0, 3, 2, 1)).reshape(128, 128, 2048)
    order = np.empty(128, dtype=np.int64)
    order[0::2] = np.arange(64)
    order[1::2] = np.arange(64) + 64
    wgu_pe = np.ascontiguousarray(gu[order])
    wd_pe = np.ascontiguousarray(
        Wd.reshape(16, 128, 4, 16, 128).transpose(2, 0, 4, 3, 1)).reshape(64, 128, 2048)
    return wq_pe, wk_pe, wv_t, wo_t, wgu_pe, wd_pe


def kernel(L0_attn_x, L0_attn_Wq, L0_attn_Wk, L0_attn_Wv, L0_attn_Wo,
           L0_ffn_w_gate_up, L0_ffn_w_down):
    x = np.ascontiguousarray(np.asarray(L0_attn_x, dtype=np.float32))
    wq_pe, wk_pe, wv_t, wo_t, wgu_pe, wd_pe = _prep_weights(
        np.asarray(L0_attn_Wq, dtype=np.float32),
        np.asarray(L0_attn_Wk, dtype=np.float32),
        np.asarray(L0_attn_Wv, dtype=np.float32),
        np.asarray(L0_attn_Wo, dtype=np.float32),
        np.asarray(L0_ffn_w_gate_up, dtype=np.float32),
        np.asarray(L0_ffn_w_down, dtype=np.float32))
    masks = _build_mask_tiles()
    id16 = np.eye(128, dtype=BF)
    id32 = np.eye(128, dtype=np.float32)

    nc = build_nc()

    in_maps = []
    for core in range(NCORE):
        seq, half = core // 2, core % 2
        if half == 1:
            xkv = x[seq]
        else:
            xkv = np.concatenate(
                [np.zeros((1024, 2048), dtype=np.float32), x[seq, :1024]], axis=0)
        par = half
        in_maps.append({
            "x_kv": np.ascontiguousarray(xkv),
            "wq_pe": wq_pe, "wk_pe": wk_pe, "wv_t": wv_t, "wo_t": wo_t,
            "wgu_pe": wgu_pe, "wd_pe": wd_pe,
            "masks": np.ascontiguousarray(masks[par]).reshape(2, 128, 13 * 512),
            "ident16": id16, "ident32": id32,
        })

    res = run_bass_kernel_spmd(nc, in_maps, core_ids=list(range(NCORE)))

    out = np.empty((4, 2048, 2048), dtype=np.float32)
    for core in range(NCORE):
        seq, half = core // 2, core % 2
        out[seq, half * 1024:(half + 1) * 1024, :] = res.results[core]["x_out"]
    return out


# version 33
# speedup vs baseline: 1.0010x; 1.0010x over previous
import math
import numpy as np
import ml_dtypes
import concourse.bass as bass
import concourse.mybir as mybir
from concourse.bass_utils import run_bass_kernel_spmd

F32 = mybir.dt.float32
BF16 = mybir.dt.bfloat16
AF = mybir.ActivationFunctionType
ALU = mybir.AluOpType
AX = mybir.AxisListType
BF = ml_dtypes.bfloat16

Bb_, T_, C_ = 4, 2048, 2048
NCORE = 8
NH, NKV, HS = 16, 4, 128
GROUP = NH // NKV
INTER = 8192
WINDOW, SINK = 1024, 4
EPS = 1e-5
SCALE = 1.0 / math.sqrt(HS)
NTB = 16
OTB = 8
NCC = 16

KL = [list(range(12)), [0] + list(range(4, 16))]
BATCHES = [
    [(0, 4), (4, 4), (8, 4)],
    [(0, 4), (4, 4), (8, 4), (12, 1)],
]

ENG = ("sync", "tensor", "vector", "scalar", "gpsimd")


class Em:

    def __init__(self):
        self.items = {e: [] for e in ENG}
        self.cnt = {e: 0 for e in ENG}
        self.dma_cnt = {}
        self.keys = {}
        self.clock = {e: {} for e in ENG}
        self.hist = {e: [None] for e in ENG}
        self.streams = set()

    def task(self, eng, fns, reads=(), writes=(), war=(), stream=None):
        needed = {}
        for k in reads:
            ent = self.keys.get(k)
            if ent is not None and ent[0] is not None:
                s, c = ent[0], ent[1]
                if needed.get(s, 0) < c:
                    needed[s] = c
        for k in tuple(writes) + tuple(war):
            ent = self.keys.get(k)
            if ent is not None:
                if ent[0] is not None and needed.get(ent[0], 0) < ent[1]:
                    needed[ent[0]] = ent[1]
                for (s, c) in ent[2]:
                    if needed.get(s, 0) < c:
                        needed[s] = c
        clock = self.clock[eng]
        waits = []
        for s in sorted(needed):
            c = needed[s]
            if clock.get(s, 0) >= c:
                continue
            waits.append((s, c))
            snap = self.hist[s][c] if c < len(self.hist[s]) else None
            if snap:
                for s2, c2 in snap.items():
                    if clock.get(s2, 0) < c2:
                        clock[s2] = c2
            if clock.get(s, 0) < c:
                clock[s] = c
        if not fns:
            self.items[eng].append((waits, [], None))
            return
        if stream is None:
            self.cnt[eng] += 1
            mycnt = self.cnt[eng]
            src_id = eng
            inc = (eng, 1)
        else:
            self.streams.add(stream)
            self.dma_cnt[stream] = self.dma_cnt.get(stream, 0) + 1
            mycnt = self.dma_cnt[stream]
            src_id = stream
            inc = (stream, 16)
            if stream not in self.hist:
                self.hist[stream] = [None]
        snap = dict(clock)
        snap[src_id] = mycnt
        h = self.hist.setdefault(src_id, [None])
        assert len(h) == mycnt, (src_id, len(h), mycnt)
        h.append(snap)
        for k in reads:
            ent = self.keys.setdefault(k, [None, 0, []])
            ent[2].append((src_id, mycnt))
        for k in writes:
            self.keys[k] = [src_id, mycnt, []]
        self.items[eng].append((waits, fns, inc))

    def wait_only(self, eng, src, cnt):
        self.items[eng].append(([(src, cnt)], [], None))

    def replay(self, eng_obj, eng_name, sems):
        for waits, fns, inc in self.items[eng_name]:
            for (s, v) in waits:
                eng_obj.wait_ge(sems[s], 16 * v if s in self.streams else v)
            last = None
            for fn in fns:
                last = fn(eng_obj)
            if inc is not None and last is not None:
                last.then_inc(sems[inc[0]], inc[1])


def _ap(t, off, dims):
    return bass.AP(t, off, dims)


def build_nc():
    nc = bass.Bass("TRN2", target_bir_lowering=False)

    x_kv = nc.dram_tensor("x_kv", [2048, 2048], F32, kind="ExternalInput").ap()
    wq = nc.dram_tensor("wq_pe", [16, 128, 2048], F32, kind="ExternalInput").ap()
    wk = nc.dram_tensor("wk_pe", [4, 128, 2048], F32, kind="ExternalInput").ap()
    wv = nc.dram_tensor("wv_t", [2048, 512], F32, kind="ExternalInput").ap()
    wo = nc.dram_tensor("wo_t", [2048, 2048], F32, kind="ExternalInput").ap()
    wgu = nc.dram_tensor("wgu_pe", [128, 128, 2048], F32, kind="ExternalInput").ap()
    wd = nc.dram_tensor("wd_pe", [64, 128, 2048], F32, kind="ExternalInput").ap()
    mk = nc.dram_tensor("masks", [2, 128, 13 * 512], BF16, kind="ExternalInput").ap()
    id16 = nc.dram_tensor("ident16", [128, 128], BF16, kind="ExternalInput").ap()
    id32 = nc.dram_tensor("ident32", [128, 128], F32, kind="ExternalInput").ap()
    x2s = nc.dram_tensor("x2s", [1024, 2048], F32).ap()
    x_out = nc.dram_tensor("x_out", [1024, 2048], F32, kind="ExternalOutput").ap()

    _probe = nc.alloc_sbuf_tensor("base_probe", [128, 32], mybir.dt.uint8)
    BASE = nc.lookup_mloc(_probe).addr + 32
    nc.alloc_sbuf_tensor("manual_region", [128, 208672], mybir.dt.uint8)

    def A(name, shape, dtype, offset):
        return nc.alloc_sbuf_tensor_at(name, shape, dtype, offset=BASE + offset)
    h1f = A("h1f", [128, 32768], BF16, offset=0)
    wo_bf = A("wo_bf", [128, 32768], BF16, offset=0)
    out_fm = A("out_fm", [128, 16384], F32, offset=0)

    qf = A("qf", [128, 16384], BF16, offset=65536)
    h2f = A("h2f", [128, 16384], BF16, offset=65536)

    xt = [A("xt0", [128, 2048], F32, offset=98304),
          A("xt1", [128, 2048], F32, offset=106496)]
    h1t = [A("h1t0", [128, 2048], BF16, offset=114688),
           A("h1t1", [128, 2048], BF16, offset=118784)]
    wqr = [A(f"wqr{i}", [128, 2048], BF16, offset=98304 + 4096 * i) for i in range(4)]
    yf = A("yf", [128, 16384], BF16, offset=98304)
    a_sb = A("a_sb", [128, 16384], BF16, offset=98304)

    kf = A("kf", [128, 8192], BF16, offset=131072)
    vt = A("vt", [128, 8192], BF16, offset=147456)
    xo = [A("xo0", [128, 2048], F32, offset=131072),
          A("xo1", [128, 2048], F32, offset=139264)]
    x2t = A("x2t", [128, 2048], F32, offset=147456)
    outT = A("outT", [128, 2048], F32, offset=155648)

    wkbf = A("wkbf", [128, 8192], BF16, offset=163840)
    wvbf = A("wvbf", [128, 8192], BF16, offset=180224)
    msk = A("msk", [128, 6656], BF16, offset=163840)
    pt = A("pt", [128, 6656], BF16, offset=177152)
    zb_sb = A("zb_sb", [128, 512], F32, offset=190464)
    zb2_sb = A("zb2_sb", [128, 512], F32, offset=192512)
    zrow2 = A("zrow2", [1, 512], BF16, offset=194560)
    wbf = ([A(f"wbf{i}", [128, 2048], BF16, offset=163840 + 4096 * i) for i in range(4)]
           + [A("wbf4", [128, 2048], BF16, offset=184320)])
    silu_t = A("silu_t", [128, 1024], F32, offset=180224)

    wst = [A("wst0", [128, 1024], F32, offset=196608),
           A("wst1", [128, 1024], F32, offset=200704)]

    ident16 = A("i16", [128, 128], BF16, offset=204800)
    ident32 = A("i32", [128, 128], F32, offset=205056)
    ones_bf = A("ones_bf", [128, 1], BF16, offset=205568)
    ones_row = A("ones_row", [1, 128], BF16, offset=205600)
    eps_sb = A("eps_sb", [128, 1], F32, offset=206112)
    bn6 = A("bn6", [128, 96], F32, offset=206144)
    bn2 = A("bn2", [128, 2], F32, offset=206528)
    rstd = A("rstd", [128, 1], F32, offset=206560)
    nmr = A("nmr", [128, 1], F32, offset=206592)
    zinv = A("zinv", [1, 512], BF16, offset=206624)
    bn6b = A("bn6b", [128, 96], F32, offset=208672)
    bn2b = A("bn2b", [128, 2], F32, offset=209056)
    rstdb = A("rstdb", [128, 1], F32, offset=209088)
    nmrb = A("nmrb", [128, 1], F32, offset=209120)
    h2tok = A("h2tok", [128, 2048], BF16, offset=180224)

    ps_a = nc.alloc_psum_tensor("ps_a", [128, 2048], F32)
    ps_b2 = nc.alloc_psum_tensor("ps_b2", [128, 2048], F32)
    pa = ps_a.ap()
    pb2 = ps_b2.ap()
    pb = pb2[:, 0:1024]
    pc = pb2[:, 1024:1536]
    pz = pb2[:, 1536:2048]
    pa_bf = pa.bitcast(BF16)
    b2_bf = pb2.bitcast(BF16)
    pb_bf = b2_bf[:, 0:2048]
    pc_bf = b2_bf[:, 2048:3072]
    pz_bf = b2_bf[:, 3072:4096]

    PA = [pa[:, 512 * i:512 * (i + 1)] for i in range(4)]
    PB = [pb[:, 512 * i:512 * (i + 1)] for i in range(2)]
    PAK = [f"pA{i}" for i in range(4)]
    PBK = [f"pB{i}" for i in range(2)]

    em = Em()
    WK = [f"wk{m}" for m in range(4)]
    WV = [f"wv{c}" for c in range(16)]
    WQR = [f"wqr{i}" for i in range(4)]
    WO = [f"wo{c}_{h}" for c in range(16) for h in range(2)]
    H1K = [f"h1_{tb}_{g}" for tb in range(16) for g in range(2)]
    QK_ = [f"q_{h}_{s}" for h in range(16) for s in range(2)]
    KK = [f"k_{m}_{t}" for m in range(4) for t in range(4)]
    VK = [f"v_{t}" for t in range(16)]
    YK = [f"y_{h}_{s}" for h in range(16) for s in range(2)]
    PTK = [f"pt{s}" for s in range(13)]
    H2K = [f"h2_{tb}" for tb in range(8)]
    AK = [f"a_{j}" for j in range(16)]
    OFK = [f"of_{m}" for m in range(16)]
    WBFK = [f"wbf{i}" for i in range(4)]

    em.task("gpsimd", [lambda g: g.memset(eps_sb.ap(), EPS)], writes=["eps"])
    em.task("gpsimd", [lambda g: g.memset(ones_bf.ap(), 1.0)], writes=["ones_bf"])
    em.task("gpsimd", [lambda g: g.memset(ones_row.ap(), 1.0)], writes=["ones_row"])
    em.task("sync", [lambda s: s.dma_start(out=ident16.ap(), in_=id16[:, :])],
            writes=["i16"], stream="cst16")
    em.task("sync", [lambda s: s.dma_start(out=ident32.ap(), in_=id32[:, :])],
            writes=["i32"], stream="cst32")

    st_ct = [0]

    def stage(src_ap, dst_ap, dst_keys, war=(), n=1024, eng="gpsimd"):
        s = st_ct[0] % 2
        st_ct[0] += 1
        slot = wst[s]
        em.task("sync",
                [lambda q, src=src_ap, d=slot, n=n: q.dma_start(out=d.ap()[:, 0:n], in_=src)],
                writes=[f"wst{s}"], stream=f"wst{s}")
        if eng == "scalar":
            fn = [lambda g, d=dst_ap, sl=slot, n=n: g.copy(d, sl.ap()[:, 0:n])]
        else:
            fn = [lambda g, d=dst_ap, sl=slot, n=n: g.tensor_copy(d, sl.ap()[:, 0:n])]
        em.task(eng, fn, reads=[f"wst{s}"], writes=list(dst_keys), war=list(war))

    wkv_stages = []
    for m in range(4):
        for h in range(2):
            wkv_stages.append(lambda m=m, h=h: stage(
                wk[m, :, 1024 * h:1024 * (h + 1)],
                wkbf.ap()[:, m * 2048 + 1024 * h: m * 2048 + 1024 * (h + 1)],
                [f"wk{m}"] if h == 1 else [f"wk{m}_half"], eng="scalar"))
    for c in range(16):
        wkv_stages.append(lambda c=c: stage(
            wv[c * 128:(c + 1) * 128, 0:512],
            wvbf.ap()[:, c * 512:(c + 1) * 512], [f"wv{c}"], n=512))

    STT = [(bn6, bn2, rstd, nmr, ""), (bn6b, bn2b, rstdb, nmrb, "b")]

    def ln_stats(tile_aps, tile_keys, n_chunks, par, emit_nmr=False):
        b6, b2, rs, nm, sx = STT[par]
        fns = []
        ci = [0]
        for ap_, w in tile_aps:
            for j in range(w // 512):
                i = ci[0]
                fns.append(lambda v, a=ap_, j=j, i=i, b6=b6: v.bn_stats(
                    b6.ap()[:, 6 * i:6 * i + 6], a[:, 512 * j:512 * (j + 1)]))
                ci[0] += 1
        assert ci[0] == n_chunks
        em.task("vector", fns, reads=list(tile_keys), writes=["bn6" + sx])
        em.task("vector", [lambda v, b6=b6, b2=b2, n=n_chunks: v.bn_aggr(
            b2.ap(), b6.ap()[:, 0:6 * n])],
                reads=["bn6" + sx], writes=["bn2" + sx])
        em.task("scalar", [lambda s, b2=b2, rs=rs: s.activation(
            rs.ap(), b2.ap()[:, 1:2], AF.Ln, bias=eps_sb.ap(), scale=1.0)],
                reads=["bn2" + sx, "eps"], writes=["rstd" + sx])
        em.task("scalar", [lambda s, rs=rs: s.activation(
            rs.ap(), rs.ap(), AF.Exp, scale=-0.5)],
                reads=["rstd" + sx], writes=["rstd" + sx])
        if emit_nmr:
            em.task("vector", [lambda v, b2=b2, rs=rs, nm=nm: v.tensor_scalar(
                nm.ap(), b2.ap()[:, 0:1], rs.ap(), -1.0, ALU.mult, ALU.mult)],
                    reads=["bn2" + sx, "rstd" + sx], writes=["nmr" + sx])


    rot = [0]

    def bank512():
        i = rot[0] % 6
        rot[0] += 1
        if i < 4:
            return PA[i], PAK[i]
        return PB[i - 4], PBK[i - 4]

    def kproj(m, ts):
        bk, bkk = bank512()
        fns = []
        for c in range(16):
            fns.append(lambda t, m=m, ts=ts, c=c, bk=bk: t.matmul(
                bk, wkbf.ap()[:, m * 2048 + c * 128:m * 2048 + (c + 1) * 128],
                h1f.ap()[:, c * 2048 + ts * 512:c * 2048 + (ts + 1) * 512],
                start=(c == 0), stop=(c == 15), skip_group_check=True))
        rd = [f"wk{m}"] + [f"h1_{4 * ts + i}_{g}" for i in range(4) for g in range(2)]
        em.task("tensor", fns, reads=rd, writes=[bkk])
        em.task("scalar",
                [lambda q, m=m, ts=ts, bk=bk: q.copy(
                    kf.ap()[:, m * 2048 + ts * 512:m * 2048 + (ts + 1) * 512], bk)],
                reads=[bkk], writes=[f"k_{m}_{ts}"])

    def vproj(tb):
        bk, bkk = bank512()
        fns = []
        for c in range(16):
            fns.append(lambda t, tb=tb, c=c, bk=bk: t.matmul(
                bk, h1f.ap()[:, c * 2048 + tb * 128:c * 2048 + (tb + 1) * 128],
                wvbf.ap()[:, c * 512:(c + 1) * 512],
                start=(c == 0), stop=(c == 15), skip_group_check=True))
        rd = [f"wv{c}" for c in range(16)] + [f"h1_{tb}_{g}" for g in range(2)]
        em.task("tensor", fns, reads=rd, writes=[bkk])
        em.task("scalar",
                [lambda q, tb=tb, bk=bk: q.copy(
                    vt.ap()[:, tb * 512:(tb + 1) * 512], bk)],
                reads=[bkk], writes=[f"v_{tb}"])

    def stage_wq(me):
        r = me % 4
        war = ["xt0", "xt1", "h1t0", "h1t1"] if me < 4 else []
        for h in range(2):
            stage(wq[me, :, 1024 * h:1024 * (h + 1)],
                  wqr[r].ap()[:, 1024 * h:1024 * (h + 1)],
                  [f"wqr{r}"] if h == 1 else [f"wqr{r}_h"],
                  war=war + ([f"wqr{r}"] if h == 0 else []), eng="scalar")

    TB_ORDER = list(range(8, 16)) + list(range(8))
    vback = []
    for idx, tb in enumerate(TB_ORDER):
        s = idx % 2
        par = idx % 2
        sx = STT[par][4]
        if idx >= 6:
            while len(vback) > 1:
                vproj(vback.pop(0))
        if 4 <= idx < 8:
            kproj(idx - 4, 2)
        elif 8 <= idx < 12:
            kproj(idx - 8, 3)
        elif 12 <= idx < 16:
            kproj(idx - 12, 0)
        em.task("gpsimd",
                [lambda q, tb=tb, s=s: q.dma_start(out=xt[s].ap(),
                                                   in_=x_kv[tb * 128:(tb + 1) * 128, :])],
                writes=[f"xt{s}"], stream=f"xt{s}")
        for _ in range(4):
            if wkv_stages:
                wkv_stages.pop(0)()
        ln_stats([(xt[s].ap(), 2048)], [f"xt{s}"], 4, par, emit_nmr=True)
        em.task("scalar",
                [lambda q, s=s, par=par: q.activation(
                    h1t[s].ap(), xt[s].ap(), AF.Identity,
                    bias=STT[par][3].ap(), scale=STT[par][2].ap())],
                reads=[f"xt{s}", "nmr" + sx, "rstd" + sx], writes=[f"h1t{s}"])
        for g in range(2):
            tbf, tbk = (pc_bf, "pC") if g == 0 else (pz_bf, "pZ")
            fns = []
            for c8 in range(8):
                fns.append(lambda t, s=s, g=g, c8=c8, tbf=tbf: t.transpose(
                    tbf[:, c8 * 128:(c8 + 1) * 128],
                    h1t[s].ap()[:, (g * 8 + c8) * 128:(g * 8 + c8 + 1) * 128],
                    ident16.ap()))
            em.task("tensor", fns, reads=[f"h1t{s}", "i16"], writes=[tbk])
        dst = _ap(h1f, tb * 128, [[32768, 128], [2048, 16], [1, 128]])
        em.task("scalar",
                [lambda q, dst=dst: q.copy(dst, b2_bf[:, 2048:4096])],
                reads=["pC", "pZ"], writes=[f"h1_{tb}_0", f"h1_{tb}_1"])
        vback.append(tb)

    while vback:
        vproj(vback.pop(0))
    for m in range(4):
        kproj(m, 1)

    stage_wq(0)
    stage_wq(1)
    for me in range(16):
        r = me % 4
        for ts in range(2):
            bk, bkk = bank512()
            fns = []
            for c in range(16):
                fns.append(lambda t, r=r, ts=ts, c=c, bk=bk: t.matmul(
                    bk, wqr[r].ap()[:, c * 128:(c + 1) * 128],
                    h1f.ap()[:, c * 2048 + 1024 + ts * 512:c * 2048 + 1024 + (ts + 1) * 512],
                    start=(c == 0), stop=(c == 15), skip_group_check=True))
            rd = [f"wqr{r}"] + [f"h1_{8 + 4 * ts + i}_{g}" for i in range(4) for g in range(2)]
            em.task("tensor", fns, reads=rd, writes=[bkk])
            em.task("scalar",
                    [lambda q, me=me, ts=ts, bk=bk: q.copy(
                        qf.ap()[:, me * 1024 + ts * 512:me * 1024 + (ts + 1) * 512], bk)],
                    reads=[bkk], writes=[f"q_{me}_{ts}"])
        if me + 2 < 16:
            stage_wq(me + 2)

    em.task("sync", [lambda s: s.dma_start(out=msk.ap(), in_=mk[0, :, :])],
            writes=["masks"], war=WK + [f"wk{m}_half" for m in range(4)], stream="msk")

    for c in range(16):
        for h in range(2):
            stage(wo[c * 128:(c + 1) * 128, 1024 * h:1024 * (h + 1)],
                  wo_bf.ap()[:, c * 2048 + 1024 * h:c * 2048 + 1024 * (h + 1)],
                  [f"wo{c}_{h}"],
                  war=H1K if (c == 0 and h == 0) else [])

    first_exp = [True]
    YB = [(pc, "pC"), (PB[0], "pB0")]
    ZB = [(pz, "pZ"), (PB[1], "pB1")]
    ZROW = [zinv, zrow2]
    ZBB = [zb_sb, zb2_sb]

    def front_batch(qs, hh, bi):
        kl = KL[qs]
        kv = hh // 4
        st0, ln = BATCHES[qs][bi]
        fns = []
        bkeys = []
        for j in range(ln):
            s_ = st0 + j
            kb = kl[s_]
            bkeys.append(PAK[j])
            fns.append(lambda t, kb=kb, kv=kv, dst=PA[j], qs=qs, hh=hh:
                       t.matmul(dst,
                                kf.ap()[:, kv * 2048 + kb * 128:kv * 2048 + (kb + 1) * 128],
                                qf.ap()[:, hh * 1024 + qs * 512:hh * 1024 + (qs + 1) * 512],
                                start=True, stop=True, skip_group_check=True))
        rd = [f"k_{kv}_{kl[st0 + j] // 4}" for j in range(ln)] + [f"q_{hh}_{qs}"]
        em.task("tensor", fns, reads=rd, writes=bkeys)
        war = (WK + WV + [f"wk{m}_half" for m in range(4)] if first_exp[0] else [])
        first_exp[0] = False
        em.task("scalar",
                [lambda q, src=pa[:, 0:512 * ln], st0=st0, ln=ln: q.activation(
                    pt.ap()[:, st0 * 512:(st0 + ln) * 512], src,
                    AF.Exp, scale=SCALE)],
                reads=bkeys, writes=[f"pt{st0 + j}" for j in range(ln)],
                war=war)
        full = set(range(5, 9)) if qs == 1 else set()
        segs = []
        j = 0
        while j < ln:
            if (st0 + j) in full:
                j += 1
                continue
            j0 = j
            while j < ln and (st0 + j) not in full:
                j += 1
            segs.append((st0 + j0, j - j0))
        if segs:
            em.task("vector",
                    [lambda v, a=a, b=b: v.tensor_tensor(
                        pt.ap()[:, a * 512:(a + b) * 512],
                        pt.ap()[:, a * 512:(a + b) * 512],
                        msk.ap()[:, a * 512:(a + b) * 512], ALU.mult)
                     for (a, b) in segs],
                    reads=["masks"] + [f"pt{st0 + j}" for j in range(ln)],
                    writes=[f"pt{st0 + j}" for j in range(ln)])

    def back_batch(qs, hh, par, bi):
        kl = KL[qs]
        kv = hh // 4
        nk = len(kl)
        st0, ln = BATCHES[qs][bi]
        yb, ybk = YB[par]
        zb, zbk = ZB[par]
        fns = []
        for j in range(ln):
            s_ = st0 + j
            kb = kl[s_]
            first = (s_ == 0)
            last = (s_ == nk - 1)
            fns.append(lambda t, kb=kb, s_=s_, kv=kv, first=first, last=last, yb=yb:
                       t.matmul(yb,
                                vt.ap()[:, kb * 512 + kv * 128:kb * 512 + (kv + 1) * 128],
                                pt.ap()[:, s_ * 512:(s_ + 1) * 512],
                                start=first, stop=last, skip_group_check=True))
        for j in range(ln):
            s_ = st0 + j
            first = (s_ == 0)
            last = (s_ == nk - 1)
            fns.append(lambda t, s_=s_, first=first, last=last, zb=zb:
                       t.matmul(zb[0:1, :], ones_bf.ap(),
                                pt.ap()[:, s_ * 512:(s_ + 1) * 512],
                                start=first, stop=last, skip_group_check=True))
        rd = [f"pt{st0 + j}" for j in range(ln)]
        rd += [f"v_{kl[st0 + j]}" for j in range(ln)] + ["ones_bf", ybk, zbk]
        em.task("tensor", fns, reads=rd, writes=[ybk, zbk])

    def back_norm_pre(qs, hh, par):
        zb, zbk = ZB[par]
        zr = ZROW[par]
        em.task("scalar", [lambda q, zr=zr, zb=zb: q.copy(zr.ap(), zb[0:1, :])],
                reads=[zbk], writes=[f"zrow{par}"])

    def back_norm(qs, hh, par):
        yb, ybk = YB[par]
        zb, zbk = ZB[par]
        zr = ZROW[par]
        zbb = ZBB[par]
        em.task("tensor",
                [lambda t, zb=zb, zr=zr: t.matmul(zb[:, :], ones_row.ap(), zr.ap(),
                                                  start=True, stop=True,
                                                  skip_group_check=True)],
                reads=[f"zrow{par}", "ones_row", zbk], writes=[zbk])
        em.task("scalar", [lambda q, zbb=zbb, zb=zb: q.copy(zbb.ap(), zb[:, :])],
                reads=[zbk], writes=[f"zb{par}"])
        em.task("scalar", [lambda q, zbb=zbb: q.activation(
            zbb.ap(), zbb.ap(), AF.Ln, scale=1.0)],
                reads=[f"zb{par}"], writes=[f"zb{par}"])
        em.task("scalar", [lambda q, zbb=zbb: q.activation(
            zbb.ap(), zbb.ap(), AF.Exp, scale=-1.0)],
                reads=[f"zb{par}"], writes=[f"zb{par}"])
        war = WQR if (qs == 0 and hh == 0) else []
        em.task("vector",
                [lambda v, hh=hh, qs=qs, yb=yb, zbb=zbb: v.tensor_tensor(
                    yf.ap()[:, hh * 1024 + qs * 512:hh * 1024 + (qs + 1) * 512],
                    yb[:, :], zbb.ap(), ALU.mult)],
                reads=[ybk, f"zb{par}"], writes=[f"y_{hh}_{qs}"], war=war)

    HSEQ = [(qs, hh) for qs in range(2) for hh in range(16)]
    for i, (qs, hh) in enumerate(HSEQ):
        if qs == 1 and hh == 0:
            em.task("sync", [lambda s: s.dma_start(out=msk.ap(), in_=mk[1, :, :])],
                    writes=["masks"], stream="msk")
        nbf = len(BATCHES[qs])
        nbb = len(BATCHES[HSEQ[i - 1][0]]) if i >= 1 else 0
        for bi in range(max(nbf, nbb)):
            if i >= 1 and bi < nbb:
                back_batch(HSEQ[i - 1][0], HSEQ[i - 1][1], (i - 1) % 2, bi)
            if bi < nbf:
                front_batch(qs, hh, bi)
            if bi == 0 and i >= 2:
                back_norm(HSEQ[i - 2][0], HSEQ[i - 2][1], (i - 2) % 2)
        if i >= 1:
            back_norm_pre(HSEQ[i - 1][0], HSEQ[i - 1][1], (i - 1) % 2)
    n = len(HSEQ)
    back_norm(HSEQ[n - 2][0], HSEQ[n - 2][1], (n - 2) % 2)
    for bi in range(len(BATCHES[HSEQ[n - 1][0]])):
        back_batch(HSEQ[n - 1][0], HSEQ[n - 1][1], (n - 1) % 2, bi)
    back_norm_pre(HSEQ[n - 1][0], HSEQ[n - 1][1], (n - 1) % 2)
    back_norm(HSEQ[n - 1][0], HSEQ[n - 1][1], (n - 1) % 2)

    pend_c = []
    for tb in range(8):
        qs = tb // 4
        s = tb % 2
        par = tb % 2
        sx = STT[par][4]
        x2b = x2t if s == 0 else outT
        xk = f"x2t_{s}"
        em.task("gpsimd",
                [lambda q, tb=tb, s=s: q.dma_start(
                    out=xo[s].ap(), in_=x_kv[(8 + tb) * 128:(9 + tb) * 128, :])],
                writes=[f"xo{s}"], war=(KK if tb < 2 else []), stream=f"xo{s}")
        if tb % 2 == 0:
            banks = [(PA[i], PAK[i]) for i in range(4)]
            tp_bf = pa_bf
            tpk = ["pA0", "pA1"]
        else:
            banks = [(PB[0], PBK[0]), (PB[1], PBK[1]), (pc, "pC"), (pz, "pZ")]
            tp_bf = pb_bf
            tpk = ["pB0", "pB1"]
        fns = []
        for c in range(16):
            for ms in range(4):
                fns.append(lambda t, c=c, ms=ms, tb=tb, bk=banks[ms][0]: t.matmul(
                    bk, yf.ap()[:, c * 1024 + tb * 128:c * 1024 + (tb + 1) * 128],
                    wo_bf.ap()[:, c * 2048 + ms * 512:c * 2048 + (ms + 1) * 512],
                    start=(c == 0), stop=(c == 15), skip_group_check=True))
        rd = [f"y_{c}_{qs}" for c in range(16)] + WO
        em.task("tensor", fns, reads=rd, writes=[bk for _, bk in banks])
        if pend_c:
            for f in pend_c.pop(0):
                f()
        ln_stats([(bk, 512) for bk, _ in banks], [bk for _, bk in banks], 4, par,
                 emit_nmr=True)
        whole = pa if tb % 2 == 0 else pb2
        em.task("scalar",
                [lambda q, whole=whole, par=par, x2b=x2b: q.activation(
                    x2b.ap(), whole, AF.Identity,
                    bias=STT[par][3].ap(), scale=STT[par][2].ap())],
                reads=[bk for _, bk in banks] + ["nmr" + sx, "rstd" + sx],
                writes=[f"x2p_{s}"],
                war=[xk] + (VK if tb < 2 else []))
        em.task("vector",
                [lambda v, s=s, x2b=x2b: v.tensor_tensor(x2b.ap(), x2b.ap(),
                                                         xo[s].ap(), ALU.add)],
                reads=[f"xo{s}", f"x2p_{s}"], writes=[xk])
        em.task("gpsimd",
                [lambda q, tb=tb, x2b=x2b: q.dma_start(
                    out=x2s[tb * 128:(tb + 1) * 128, :], in_=x2b.ap())],
                reads=[xk], writes=[f"x2s_{tb}"], stream=f"x2w{s}")
        ln_stats([(x2b.ap(), 2048)], [xk], 4, par, emit_nmr=True)
        em.task("scalar",
                [lambda q, par=par, x2b=x2b: q.activation(
                    h2tok.ap(), x2b.ap(), AF.Identity,
                    bias=STT[par][3].ap(), scale=STT[par][2].ap())],
                reads=[xk, "nmr" + sx, "rstd" + sx], writes=["h2tok"],
                war=((VK + PTK + WV) if tb == 0 else []))

        def emit_tp(tb=tb, tp_bf=tp_bf, tpk=tpk):
            fns = []
            for c in range(16):
                fns.append(lambda t, c=c, tp_bf=tp_bf: t.transpose(
                    tp_bf[:, c * 128:(c + 1) * 128],
                    h2tok.ap()[:, c * 128:(c + 1) * 128], ident16.ap()))
            em.task("tensor", fns, reads=["h2tok", "i16"], writes=tpk)

        def emit_ev(tb=tb, tp_bf=tp_bf, tpk=tpk):
            dst = _ap(h2f, tb * 128, [[16384, 128], [1024, 16], [1, 128]])
            em.task("scalar",
                    [lambda q, dst=dst, tp_bf=tp_bf: q.copy(dst, tp_bf[:, 0:2048])],
                    reads=tpk, writes=[f"h2_{tb}"],
                    war=(QK_ if tb == 0 else []))
        pend_c.append([emit_tp, emit_ev])
    while pend_c:
        for f in pend_c.pop(0):
            f()

    went = [0]
    wbf_used = set()

    def wslot():
        r = went[0] % 5
        went[0] += 1
        return r

    for q in range(4):
        for jj in range(16):
            j = q * 16 + jj
            rg = wslot()
            ru = wslot()
            war_g = (PTK + ["masks"] if rg not in wbf_used else [])
            wbf_used.add(rg)
            for h in range(2):
                stage(wgu[2 * j, :, 1024 * h:1024 * (h + 1)],
                      wbf[rg].ap()[:, 1024 * h:1024 * (h + 1)],
                      [f"wbf{rg}"] if h == 1 else [f"wbf{rg}_h"],
                      war=war_g + ([f"wbf{rg}"] if h == 0 else []), eng="scalar")
            war_u = (PTK + ["masks"] if ru not in wbf_used else [])
            wbf_used.add(ru)
            for h in range(2):
                stage(wgu[2 * j + 1, :, 1024 * h:1024 * (h + 1)],
                      wbf[ru].ap()[:, 1024 * h:1024 * (h + 1)],
                      [f"wbf{ru}"] if h == 1 else [f"wbf{ru}_h"],
                      war=war_u + ([f"wbf{ru}"] if h == 0 else []), eng="scalar")
            fns = []
            for c in range(16):
                for ts in range(2):
                    fns.append(lambda t, c=c, ts=ts, rg=rg: t.matmul(
                        PB[ts], wbf[rg].ap()[:, c * 128:(c + 1) * 128],
                        h2f.ap()[:, c * 1024 + ts * 512:c * 1024 + (ts + 1) * 512],
                        start=(c == 0), stop=(c == 15), skip_group_check=True))
            em.task("tensor", fns, reads=[f"wbf{rg}"] + H2K, writes=PBK)
            fns = []
            for c in range(16):
                for ts in range(2):
                    fns.append(lambda t, c=c, ts=ts, ru=ru: t.matmul(
                        pa[:, ts * 512:(ts + 1) * 512],
                        wbf[ru].ap()[:, c * 128:(c + 1) * 128],
                        h2f.ap()[:, c * 1024 + ts * 512:c * 1024 + (ts + 1) * 512],
                        start=(c == 0), stop=(c == 15), skip_group_check=True))
            em.task("tensor", fns, reads=[f"wbf{ru}"] + H2K,
                    writes=["pA0", "pA1"])
            em.task("scalar",
                    [lambda q_: q_.activation(silu_t.ap(), pb[:, :], AF.Silu)],
                    reads=PBK, writes=["silu"],
                    war=((PTK + ["h2tok"]) if (q == 0 and jj == 0) else []))
            em.task("vector",
                    [lambda v, jj=jj: v.tensor_tensor(
                        a_sb.ap()[:, jj * 1024:(jj + 1) * 1024],
                        silu_t.ap(), pa[:, 0:1024], ALU.mult)],
                    reads=["silu", "pA0", "pA1"], writes=[f"a_{jj}"],
                    war=(YK if (q == 0 and jj == 0) else []))
        for mc in range(16):
            r = wslot()
            for h in range(2):
                stage(wd[q * 16 + mc, :, 1024 * h:1024 * (h + 1)],
                      wbf[r].ap()[:, 1024 * h:1024 * (h + 1)],
                      [f"wbf{r}"] if h == 1 else [f"wbf{r}_h"],
                      war=([f"wbf{r}"] if h == 0 else []), eng="scalar")
            if mc % 2 == 0:
                dps = [pa[:, 1024:1536], pa[:, 1536:2048]]
                dpk = ["pA2", "pA3"]
            else:
                dps = [pz, pc]
                dpk = ["pZ", "pC"]
            fns = []
            for ic in range(16):
                for ts in range(2):
                    fns.append(lambda t, ic=ic, ts=ts, r=r, dps=dps: t.matmul(
                        dps[ts], wbf[r].ap()[:, ic * 128:(ic + 1) * 128],
                        a_sb.ap()[:, ic * 1024 + ts * 512:ic * 1024 + (ts + 1) * 512],
                        start=(ic == 0), stop=(ic == 15), skip_group_check=True))
            em.task("tensor", fns, reads=[f"wbf{r}"] + AK, writes=dpk)
            if q == 0:
                war = WO if mc == 0 else []
                em.task("vector",
                        [lambda s_, mc=mc, dps=dps: s_.tensor_copy(
                            out_fm.ap()[:, mc * 1024:mc * 1024 + 512], dps[0]),
                         lambda s_, mc=mc, dps=dps: s_.tensor_copy(
                            out_fm.ap()[:, mc * 1024 + 512:(mc + 1) * 1024], dps[1])],
                        reads=dpk, writes=[f"of_{mc}"], war=war)
            else:
                em.task("vector",
                        [lambda v, mc=mc, dps=dps: v.tensor_tensor(
                            out_fm.ap()[:, mc * 1024:mc * 1024 + 512],
                            out_fm.ap()[:, mc * 1024:mc * 1024 + 512],
                            dps[0], ALU.add),
                         lambda v, mc=mc, dps=dps: v.tensor_tensor(
                            out_fm.ap()[:, mc * 1024 + 512:(mc + 1) * 1024],
                            out_fm.ap()[:, mc * 1024 + 512:(mc + 1) * 1024],
                            dps[1], ALU.add)],
                        reads=dpk + [f"of_{mc}"], writes=[f"of_{mc}"])

    em.task("sync", [], reads=[f"x2s_{t}" for t in range(8)])

    def e_stage1(tb):
        s = tb % 2
        par = tb % 2
        sx = STT[par][4]
        if s == 1:
            TPP = [(pb2[:, 0:512], "pB0"), (pb2[:, 512:1024], "pB1"),
                   (pb2[:, 1024:1536], "pC"), (pb2[:, 1536:2048], "pZ")]
            tsrc, tkeys = pb2, ["pB0", "pB1", "pC", "pZ"]
        else:
            TPP = [(pa[:, 512 * g:512 * (g + 1)], PAK[g]) for g in range(4)]
            tsrc, tkeys = pa, list(PAK)
        x2buf = x2t if s == 0 else xo[0]
        x2key = "x2t" if s == 0 else "xo0"
        otbuf = outT if s == 0 else xo[1]
        otkey = "outT" if s == 0 else "xo1"
        war_r = [x2key + "fin"]
        if s == 0:
            war_r += ["x2t_0", "x2p_0", "h2tok"]
        em.task("sync",
                [lambda q, tb=tb, x2buf=x2buf: q.dma_start(
                    out=x2buf.ap(), in_=x2s[tb * 128:(tb + 1) * 128, :])],
                reads=[f"x2s_{tb}"], writes=[x2key], war=war_r, stream=f"x2r{s}")
        for g in range(4):
            tpp, tppk = TPP[g]
            fns = []
            for k in range(4):
                mc = g * 4 + k
                fns.append(lambda t, mc=mc, k=k, tb=tb, tpp=tpp: t.transpose(
                    tpp[:, k * 128:(k + 1) * 128],
                    out_fm.ap()[:, mc * 1024 + tb * 128:mc * 1024 + (tb + 1) * 128],
                    ident32.ap()))
            em.task("tensor", fns,
                    reads=[f"of_{g * 4 + k}" for k in range(4)] + ["i32"],
                    writes=[tppk])
        b6, b2_, rs, nm, _ = STT[par]
        fns = []
        for j in range(4):
            fns.append(lambda v, j=j, b6=b6, tsrc=tsrc: v.bn_stats(
                b6.ap()[:, 6 * j:6 * j + 6],
                tsrc[:, 512 * j:512 * (j + 1)]))
        em.task("vector", fns, reads=tkeys, writes=["bn6" + sx])
        em.task("vector", [lambda v, b6=b6, b2_=b2_: v.bn_aggr(
            b2_.ap(), b6.ap()[:, 0:24])],
                reads=["bn6" + sx], writes=["bn2" + sx])

    def e_stage2(tb):
        s = tb % 2
        par = tb % 2
        sx = STT[par][4]
        x2buf = x2t if s == 0 else xo[0]
        x2key = "x2t" if s == 0 else "xo0"
        otbuf = outT if s == 0 else xo[1]
        otkey = "outT" if s == 0 else "xo1"
        if s == 1:
            tsrc, tkeys = pb2, ["pB0", "pB1", "pC", "pZ"]
        else:
            tsrc, tkeys = pa, list(PAK)
        b6, b2_, rs, nm, _ = STT[par]
        em.task("scalar", [lambda q, b2_=b2_, rs=rs: q.activation(
            rs.ap(), b2_.ap()[:, 1:2], AF.Ln, bias=eps_sb.ap(), scale=1.0)],
                reads=["bn2" + sx, "eps"], writes=["rstd" + sx])
        em.task("scalar", [lambda q, rs=rs: q.activation(
            rs.ap(), rs.ap(), AF.Exp, scale=-0.5)],
                reads=["rstd" + sx], writes=["rstd" + sx])
        em.task("vector", [lambda v, b2_=b2_, rs=rs, nm=nm: v.tensor_scalar(
            nm.ap(), b2_.ap()[:, 0:1], rs.ap(), -1.0, ALU.mult, ALU.mult)],
                reads=["bn2" + sx, "rstd" + sx], writes=["nmr" + sx])
        war_o = [otkey + "ln"] + (["xo1"] if s == 1 else
                                  ["h2tok", "x2t_1", "x2p_1"])
        em.task("scalar",
                [lambda q, otbuf=otbuf, par=par, tsrc=tsrc: q.activation(
                    otbuf.ap(), tsrc, AF.Identity,
                    bias=STT[par][3].ap(), scale=STT[par][2].ap())],
                reads=tkeys + ["nmr" + sx, "rstd" + sx],
                writes=[otkey + "ln"], war=war_o)
        em.task("vector",
                [lambda v, x2buf=x2buf, otbuf=otbuf: v.tensor_tensor(
                    x2buf.ap(), x2buf.ap(), otbuf.ap(), ALU.add)],
                reads=[x2key, otkey + "ln"], writes=[x2key + "fin"])
        em.task("sync",
                [lambda q, tb=tb, x2buf=x2buf: q.dma_start(
                    out=x_out[tb * 128:(tb + 1) * 128, :], in_=x2buf.ap())],
                reads=[x2key + "fin"], writes=[x2key], stream=f"out{tb}")

    for tb in range(8):
        e_stage1(tb)
        if tb >= 1:
            e_stage2(tb - 1)
    e_stage2(7)

    for tb in range(8):
        em.wait_only("scalar", f"out{tb}", 1)
        em.wait_only("sync", f"out{tb}", 1)

    sems = {}
    sem_names = list(ENG) + sorted(em.streams)
    for s in sem_names:
        sems[s] = nc.alloc_semaphore(name=f"s_{s}")

    with nc.Block() as block:
        @block.sync
        def _(e):
            em.replay(e, "sync", sems)

        @block.tensor
        def _(e):
            em.replay(e, "tensor", sems)

        @block.vector
        def _(e):
            em.replay(e, "vector", sems)

        @block.scalar
        def _(e):
            em.replay(e, "scalar", sems)

        @block.gpsimd
        def _(e):
            em.replay(e, "gpsimd", sems)

    return nc



def _build_mask_tiles():
    T = 2048
    i = np.arange(T)[:, None]
    j = np.arange(T)[None, :]
    causal = j <= i
    band = j >= i - (WINDOW - 1)
    m = causal & band
    m = m | (j < SINK)
    m = m & ((i >= SINK) | causal)
    out = np.zeros((2, 2, 128, 13 * 512), dtype=BF)
    for par in range(2):
        off = 1024 if par == 0 else 0
        F = np.zeros((2048, 2048), dtype=bool)
        if par == 0:
            F[1024:, 1024:] = m[:1024, :1024]
        else:
            F[1024:, :] = m[1024:, :]
        for qs in range(2):
            q0 = 1024 + qs * 512
            for s_, kb in enumerate(KL[qs]):
                tile = F[q0:q0 + 512, kb * 128:(kb + 1) * 128].T
                out[par, qs, :, s_ * 512:(s_ + 1) * 512] = tile.astype(BF)
    return out


def _prep_weights(Wq, Wk, Wv, Wo, Wgu, Wd):
    wq_pe = np.ascontiguousarray(
        Wq.reshape(16, 128, 16, 128).transpose(0, 3, 2, 1)).reshape(16, 128, 2048)
    wk_pe = np.ascontiguousarray(
        Wk.reshape(4, 128, 16, 128).transpose(0, 3, 2, 1)).reshape(4, 128, 2048)
    wv_t = np.ascontiguousarray(Wv.T)
    wo_t = np.ascontiguousarray(Wo.T)
    gu = np.ascontiguousarray(
        Wgu.reshape(128, 128, 16, 128).transpose(0, 3, 2, 1)).reshape(128, 128, 2048)
    order = np.empty(128, dtype=np.int64)
    order[0::2] = np.arange(64)
    order[1::2] = np.arange(64) + 64
    wgu_pe = np.ascontiguousarray(gu[order])
    wd_pe = np.ascontiguousarray(
        Wd.reshape(16, 128, 4, 16, 128).transpose(2, 0, 4, 3, 1)).reshape(64, 128, 2048)
    return wq_pe, wk_pe, wv_t, wo_t, wgu_pe, wd_pe


def kernel(L0_attn_x, L0_attn_Wq, L0_attn_Wk, L0_attn_Wv, L0_attn_Wo,
           L0_ffn_w_gate_up, L0_ffn_w_down):
    x = np.ascontiguousarray(np.asarray(L0_attn_x, dtype=np.float32))
    wq_pe, wk_pe, wv_t, wo_t, wgu_pe, wd_pe = _prep_weights(
        np.asarray(L0_attn_Wq, dtype=np.float32),
        np.asarray(L0_attn_Wk, dtype=np.float32),
        np.asarray(L0_attn_Wv, dtype=np.float32),
        np.asarray(L0_attn_Wo, dtype=np.float32),
        np.asarray(L0_ffn_w_gate_up, dtype=np.float32),
        np.asarray(L0_ffn_w_down, dtype=np.float32))
    masks = _build_mask_tiles()
    id16 = np.eye(128, dtype=BF)
    id32 = np.eye(128, dtype=np.float32)

    nc = build_nc()

    in_maps = []
    for core in range(NCORE):
        seq, half = core // 2, core % 2
        if half == 1:
            xkv = x[seq]
        else:
            xkv = np.concatenate(
                [np.zeros((1024, 2048), dtype=np.float32), x[seq, :1024]], axis=0)
        par = half
        in_maps.append({
            "x_kv": np.ascontiguousarray(xkv),
            "wq_pe": wq_pe, "wk_pe": wk_pe, "wv_t": wv_t, "wo_t": wo_t,
            "wgu_pe": wgu_pe, "wd_pe": wd_pe,
            "masks": np.ascontiguousarray(masks[par]).reshape(2, 128, 13 * 512),
            "ident16": id16, "ident32": id32,
        })

    res = run_bass_kernel_spmd(nc, in_maps, core_ids=list(range(NCORE)))

    out = np.empty((4, 2048, 2048), dtype=np.float32)
    for core in range(NCORE):
        seq, half = core // 2, core % 2
        out[seq, half * 1024:(half + 1) * 1024, :] = res.results[core]["x_out"]
    return out


# version 34 (final)
# speedup vs baseline: 1.0069x; 1.0058x over previous
import math
import numpy as np
import ml_dtypes
import concourse.bass as bass
import concourse.mybir as mybir
from concourse.bass_utils import run_bass_kernel_spmd

F32 = mybir.dt.float32
BF16 = mybir.dt.bfloat16
AF = mybir.ActivationFunctionType
ALU = mybir.AluOpType
AX = mybir.AxisListType
BF = ml_dtypes.bfloat16

Bb_, T_, C_ = 4, 2048, 2048
NCORE = 8
NH, NKV, HS = 16, 4, 128
GROUP = NH // NKV
INTER = 8192
WINDOW, SINK = 1024, 4
EPS = 1e-5
SCALE = 1.0 / math.sqrt(HS)
NTB = 16
OTB = 8
NCC = 16

KL = [list(range(12)), [0] + list(range(4, 16))]
BATCHES = [
    [(0, 4), (4, 4), (8, 4)],
    [(0, 4), (4, 4), (8, 4), (12, 1)],
]

ENG = ("sync", "tensor", "vector", "scalar", "gpsimd")


class Em:

    def __init__(self):
        self.items = {e: [] for e in ENG}
        self.cnt = {e: 0 for e in ENG}
        self.dma_cnt = {}
        self.keys = {}
        self.clock = {e: {} for e in ENG}
        self.hist = {e: [None] for e in ENG}
        self.streams = set()

    def task(self, eng, fns, reads=(), writes=(), war=(), stream=None):
        needed = {}
        for k in reads:
            ent = self.keys.get(k)
            if ent is not None and ent[0] is not None:
                s, c = ent[0], ent[1]
                if needed.get(s, 0) < c:
                    needed[s] = c
        for k in tuple(writes) + tuple(war):
            ent = self.keys.get(k)
            if ent is not None:
                if ent[0] is not None and needed.get(ent[0], 0) < ent[1]:
                    needed[ent[0]] = ent[1]
                for (s, c) in ent[2]:
                    if needed.get(s, 0) < c:
                        needed[s] = c
        clock = self.clock[eng]
        waits = []
        for s in sorted(needed):
            c = needed[s]
            if clock.get(s, 0) >= c:
                continue
            waits.append((s, c))
            snap = self.hist[s][c] if c < len(self.hist[s]) else None
            if snap:
                for s2, c2 in snap.items():
                    if clock.get(s2, 0) < c2:
                        clock[s2] = c2
            if clock.get(s, 0) < c:
                clock[s] = c
        if not fns:
            self.items[eng].append((waits, [], None))
            return
        if stream is None:
            self.cnt[eng] += 1
            mycnt = self.cnt[eng]
            src_id = eng
            inc = (eng, 1)
        else:
            self.streams.add(stream)
            self.dma_cnt[stream] = self.dma_cnt.get(stream, 0) + 1
            mycnt = self.dma_cnt[stream]
            src_id = stream
            inc = (stream, 16)
            if stream not in self.hist:
                self.hist[stream] = [None]
        snap = dict(clock)
        snap[src_id] = mycnt
        h = self.hist.setdefault(src_id, [None])
        assert len(h) == mycnt, (src_id, len(h), mycnt)
        h.append(snap)
        for k in reads:
            ent = self.keys.setdefault(k, [None, 0, []])
            ent[2].append((src_id, mycnt))
        for k in writes:
            self.keys[k] = [src_id, mycnt, []]
        self.items[eng].append((waits, fns, inc))

    def wait_only(self, eng, src, cnt):
        self.items[eng].append(([(src, cnt)], [], None))

    def replay(self, eng_obj, eng_name, sems):
        for waits, fns, inc in self.items[eng_name]:
            for (s, v) in waits:
                eng_obj.wait_ge(sems[s], 16 * v if s in self.streams else v)
            last = None
            for fn in fns:
                last = fn(eng_obj)
            if inc is not None and last is not None:
                last.then_inc(sems[inc[0]], inc[1])


def _ap(t, off, dims):
    return bass.AP(t, off, dims)


def build_nc():
    nc = bass.Bass("TRN2", target_bir_lowering=False)

    x_kv = nc.dram_tensor("x_kv", [2048, 2048], F32, kind="ExternalInput").ap()
    wq = nc.dram_tensor("wq_pe", [16, 128, 2048], F32, kind="ExternalInput").ap()
    wk = nc.dram_tensor("wk_pe", [4, 128, 2048], F32, kind="ExternalInput").ap()
    wv = nc.dram_tensor("wv_t", [2048, 512], F32, kind="ExternalInput").ap()
    wo = nc.dram_tensor("wo_t", [2048, 2048], F32, kind="ExternalInput").ap()
    wgu = nc.dram_tensor("wgu_pe", [128, 128, 2048], F32, kind="ExternalInput").ap()
    wd = nc.dram_tensor("wd_pe", [64, 128, 2048], F32, kind="ExternalInput").ap()
    mk = nc.dram_tensor("masks", [2, 128, 13 * 512], BF16, kind="ExternalInput").ap()
    id16 = nc.dram_tensor("ident16", [128, 128], BF16, kind="ExternalInput").ap()
    id32 = nc.dram_tensor("ident32", [128, 128], F32, kind="ExternalInput").ap()
    x2s = nc.dram_tensor("x2s", [1024, 2048], F32).ap()
    x_out = nc.dram_tensor("x_out", [1024, 2048], F32, kind="ExternalOutput").ap()

    _probe = nc.alloc_sbuf_tensor("base_probe", [128, 32], mybir.dt.uint8)
    BASE = nc.lookup_mloc(_probe).addr + 32
    nc.alloc_sbuf_tensor("manual_region", [128, 208672], mybir.dt.uint8)

    def A(name, shape, dtype, offset):
        return nc.alloc_sbuf_tensor_at(name, shape, dtype, offset=BASE + offset)
    h1f = A("h1f", [128, 32768], BF16, offset=0)
    wo_bf = A("wo_bf", [128, 32768], BF16, offset=0)
    out_fm = A("out_fm", [128, 16384], F32, offset=0)

    qf = A("qf", [128, 16384], BF16, offset=65536)
    h2f = A("h2f", [128, 16384], BF16, offset=65536)

    xt = [A("xt0", [128, 2048], F32, offset=98304),
          A("xt1", [128, 2048], F32, offset=106496)]
    h1t = [A("h1t0", [128, 2048], BF16, offset=114688),
           A("h1t1", [128, 2048], BF16, offset=118784)]
    wqr = [A(f"wqr{i}", [128, 2048], BF16, offset=98304 + 4096 * i) for i in range(4)]
    yf = A("yf", [128, 16384], BF16, offset=98304)
    a_sb = A("a_sb", [128, 16384], BF16, offset=98304)

    kf = A("kf", [128, 8192], BF16, offset=131072)
    vt = A("vt", [128, 8192], BF16, offset=147456)
    xo = [A("xo0", [128, 2048], F32, offset=131072),
          A("xo1", [128, 2048], F32, offset=139264)]
    x2t = A("x2t", [128, 2048], F32, offset=147456)
    outT = A("outT", [128, 2048], F32, offset=155648)

    wkbf = A("wkbf", [128, 8192], BF16, offset=163840)
    wvbf = A("wvbf", [128, 8192], BF16, offset=180224)
    msk = A("msk", [128, 6656], BF16, offset=163840)
    pt = A("pt", [128, 6656], BF16, offset=177152)
    zb_sb = A("zb_sb", [128, 512], F32, offset=190464)
    zb2_sb = A("zb2_sb", [128, 512], F32, offset=192512)
    zrow2 = A("zrow2", [1, 512], BF16, offset=194560)
    wbf = ([A(f"wbf{i}", [128, 2048], BF16, offset=163840 + 4096 * i) for i in range(4)]
           + [A("wbf4", [128, 2048], BF16, offset=184320)])
    silu_t = A("silu_t", [128, 1024], F32, offset=180224)

    wst = [A("wst0", [128, 1024], F32, offset=196608),
           A("wst1", [128, 1024], F32, offset=200704)]

    ident16 = A("i16", [128, 128], BF16, offset=204800)
    ident32 = A("i32", [128, 128], F32, offset=205056)
    ones_bf = A("ones_bf", [128, 1], BF16, offset=205568)
    ones_row = A("ones_row", [1, 128], BF16, offset=205600)
    eps_sb = A("eps_sb", [128, 1], F32, offset=206112)
    bn6 = A("bn6", [128, 96], F32, offset=206144)
    bn2 = A("bn2", [128, 2], F32, offset=206528)
    rstd = A("rstd", [128, 1], F32, offset=206560)
    nmr = A("nmr", [128, 1], F32, offset=206592)
    zinv = A("zinv", [1, 512], BF16, offset=206624)
    bn6b = A("bn6b", [128, 96], F32, offset=208672)
    bn2b = A("bn2b", [128, 2], F32, offset=209056)
    rstdb = A("rstdb", [128, 1], F32, offset=209088)
    nmrb = A("nmrb", [128, 1], F32, offset=209120)
    h2tok = A("h2tok", [128, 2048], BF16, offset=180224)

    ps_a = nc.alloc_psum_tensor("ps_a", [128, 2048], F32)
    ps_b2 = nc.alloc_psum_tensor("ps_b2", [128, 2048], F32)
    pa = ps_a.ap()
    pb2 = ps_b2.ap()
    pb = pb2[:, 0:1024]
    pc = pb2[:, 1024:1536]
    pz = pb2[:, 1536:2048]
    pa_bf = pa.bitcast(BF16)
    b2_bf = pb2.bitcast(BF16)
    pb_bf = b2_bf[:, 0:2048]
    pc_bf = b2_bf[:, 2048:3072]
    pz_bf = b2_bf[:, 3072:4096]

    PA = [pa[:, 512 * i:512 * (i + 1)] for i in range(4)]
    PB = [pb[:, 512 * i:512 * (i + 1)] for i in range(2)]
    PAK = [f"pA{i}" for i in range(4)]
    PBK = [f"pB{i}" for i in range(2)]

    em = Em()
    WK = [f"wk{m}" for m in range(4)]
    WV = [f"wv{c}" for c in range(16)]
    WQR = [f"wqr{i}" for i in range(4)]
    WO = [f"wo{c}_{h}" for c in range(16) for h in range(2)]
    H1K = [f"h1_{tb}_{g}" for tb in range(16) for g in range(2)]
    QK_ = [f"q_{h}_{s}" for h in range(16) for s in range(2)]
    KK = [f"k_{m}_{t}" for m in range(4) for t in range(4)]
    VK = [f"v_{t}" for t in range(16)]
    YK = [f"y_{h}_{s}" for h in range(16) for s in range(2)]
    PTK = [f"pt{s}" for s in range(13)]
    H2K = [f"h2_{tb}" for tb in range(8)]
    AK = [f"a_{j}" for j in range(16)]
    OFK = [f"of_{m}" for m in range(16)]
    WBFK = [f"wbf{i}" for i in range(4)]

    em.task("gpsimd", [lambda g: g.memset(eps_sb.ap(), EPS)], writes=["eps"])
    em.task("gpsimd", [lambda g: g.memset(ones_bf.ap(), 1.0)], writes=["ones_bf"])
    em.task("gpsimd", [lambda g: g.memset(ones_row.ap(), 1.0)], writes=["ones_row"])
    em.task("sync", [lambda s: s.dma_start(out=ident16.ap(), in_=id16[:, :])],
            writes=["i16"], stream="cst16")
    em.task("sync", [lambda s: s.dma_start(out=ident32.ap(), in_=id32[:, :])],
            writes=["i32"], stream="cst32")

    st_ct = [0]

    def stage(src_ap, dst_ap, dst_keys, war=(), n=1024, eng="gpsimd"):
        s = st_ct[0] % 2
        st_ct[0] += 1
        slot = wst[s]
        em.task("sync",
                [lambda q, src=src_ap, d=slot, n=n: q.dma_start(out=d.ap()[:, 0:n], in_=src)],
                writes=[f"wst{s}"], stream=f"wst{s}")
        if eng == "scalar":
            fn = [lambda g, d=dst_ap, sl=slot, n=n: g.copy(d, sl.ap()[:, 0:n])]
        else:
            fn = [lambda g, d=dst_ap, sl=slot, n=n: g.tensor_copy(d, sl.ap()[:, 0:n])]
        em.task(eng, fn, reads=[f"wst{s}"], writes=list(dst_keys), war=list(war))

    wkv_stages = []
    for m in range(4):
        for h in range(2):
            wkv_stages.append(lambda m=m, h=h: stage(
                wk[m, :, 1024 * h:1024 * (h + 1)],
                wkbf.ap()[:, m * 2048 + 1024 * h: m * 2048 + 1024 * (h + 1)],
                [f"wk{m}"] if h == 1 else [f"wk{m}_half"], eng="scalar"))
    for c in range(16):
        wkv_stages.append(lambda c=c: stage(
            wv[c * 128:(c + 1) * 128, 0:512],
            wvbf.ap()[:, c * 512:(c + 1) * 512], [f"wv{c}"], n=512))

    STT = [(bn6, bn2, rstd, nmr, ""), (bn6b, bn2b, rstdb, nmrb, "b")]

    def ln_stats(tile_aps, tile_keys, n_chunks, par, emit_nmr=False):
        b6, b2, rs, nm, sx = STT[par]
        fns = []
        ci = [0]
        for ap_, w in tile_aps:
            for j in range(w // 512):
                i = ci[0]
                fns.append(lambda v, a=ap_, j=j, i=i, b6=b6: v.bn_stats(
                    b6.ap()[:, 6 * i:6 * i + 6], a[:, 512 * j:512 * (j + 1)]))
                ci[0] += 1
        assert ci[0] == n_chunks
        em.task("vector", fns, reads=list(tile_keys), writes=["bn6" + sx])
        em.task("vector", [lambda v, b6=b6, b2=b2, n=n_chunks: v.bn_aggr(
            b2.ap(), b6.ap()[:, 0:6 * n])],
                reads=["bn6" + sx], writes=["bn2" + sx])
        em.task("scalar", [lambda s, b2=b2, rs=rs: s.activation(
            rs.ap(), b2.ap()[:, 1:2], AF.Ln, bias=eps_sb.ap(), scale=1.0)],
                reads=["bn2" + sx, "eps"], writes=["rstd" + sx])
        em.task("scalar", [lambda s, rs=rs: s.activation(
            rs.ap(), rs.ap(), AF.Exp, scale=-0.5)],
                reads=["rstd" + sx], writes=["rstd" + sx])
        if emit_nmr:
            em.task("vector", [lambda v, b2=b2, rs=rs, nm=nm: v.tensor_scalar(
                nm.ap(), b2.ap()[:, 0:1], rs.ap(), -1.0, ALU.mult, ALU.mult)],
                    reads=["bn2" + sx, "rstd" + sx], writes=["nmr" + sx])


    rot = [0]

    def bank512():
        i = rot[0] % 6
        rot[0] += 1
        if i < 4:
            return PA[i], PAK[i]
        return PB[i - 4], PBK[i - 4]

    def kproj(m, ts):
        bk, bkk = bank512()
        fns = []
        for c in range(16):
            fns.append(lambda t, m=m, ts=ts, c=c, bk=bk: t.matmul(
                bk, wkbf.ap()[:, m * 2048 + c * 128:m * 2048 + (c + 1) * 128],
                h1f.ap()[:, c * 2048 + ts * 512:c * 2048 + (ts + 1) * 512],
                start=(c == 0), stop=(c == 15), skip_group_check=True))
        rd = [f"wk{m}"] + [f"h1_{4 * ts + i}_{g}" for i in range(4) for g in range(2)]
        em.task("tensor", fns, reads=rd, writes=[bkk])
        em.task("scalar",
                [lambda q, m=m, ts=ts, bk=bk: q.copy(
                    kf.ap()[:, m * 2048 + ts * 512:m * 2048 + (ts + 1) * 512], bk)],
                reads=[bkk], writes=[f"k_{m}_{ts}"])

    def vproj(tb):
        bk, bkk = bank512()
        fns = []
        for c in range(16):
            fns.append(lambda t, tb=tb, c=c, bk=bk: t.matmul(
                bk, h1f.ap()[:, c * 2048 + tb * 128:c * 2048 + (tb + 1) * 128],
                wvbf.ap()[:, c * 512:(c + 1) * 512],
                start=(c == 0), stop=(c == 15), skip_group_check=True))
        rd = [f"wv{c}" for c in range(16)] + [f"h1_{tb}_{g}" for g in range(2)]
        em.task("tensor", fns, reads=rd, writes=[bkk])
        em.task("scalar",
                [lambda q, tb=tb, bk=bk: q.copy(
                    vt.ap()[:, tb * 512:(tb + 1) * 512], bk)],
                reads=[bkk], writes=[f"v_{tb}"])

    def stage_wq(me):
        r = me % 4
        war = ["xt0", "xt1", "h1t0", "h1t1"] if me < 4 else []
        for h in range(2):
            stage(wq[me, :, 1024 * h:1024 * (h + 1)],
                  wqr[r].ap()[:, 1024 * h:1024 * (h + 1)],
                  [f"wqr{r}"] if h == 1 else [f"wqr{r}_h"],
                  war=war + ([f"wqr{r}"] if h == 0 else []), eng="scalar")

    TB_ORDER = list(range(8, 16)) + list(range(8))
    vback = []
    for idx, tb in enumerate(TB_ORDER):
        s = idx % 2
        par = idx % 2
        sx = STT[par][4]
        if idx >= 6:
            while len(vback) > 1:
                vproj(vback.pop(0))
        if idx == 4:
            for m in range(4):
                kproj(m, 2)
        elif idx == 8:
            for m in range(4):
                kproj(m, 3)
        elif idx == 12:
            for m in range(4):
                kproj(m, 0)
        em.task("gpsimd",
                [lambda q, tb=tb, s=s: q.dma_start(out=xt[s].ap(),
                                                   in_=x_kv[tb * 128:(tb + 1) * 128, :])],
                writes=[f"xt{s}"], stream=f"xt{s}")
        for _ in range(4):
            if wkv_stages:
                wkv_stages.pop(0)()
        ln_stats([(xt[s].ap(), 2048)], [f"xt{s}"], 4, par, emit_nmr=True)
        em.task("scalar",
                [lambda q, s=s, par=par: q.activation(
                    h1t[s].ap(), xt[s].ap(), AF.Identity,
                    bias=STT[par][3].ap(), scale=STT[par][2].ap())],
                reads=[f"xt{s}", "nmr" + sx, "rstd" + sx], writes=[f"h1t{s}"])
        for g in range(2):
            tbf, tbk = (pc_bf, "pC") if g == 0 else (pz_bf, "pZ")
            fns = []
            for c8 in range(8):
                fns.append(lambda t, s=s, g=g, c8=c8, tbf=tbf: t.transpose(
                    tbf[:, c8 * 128:(c8 + 1) * 128],
                    h1t[s].ap()[:, (g * 8 + c8) * 128:(g * 8 + c8 + 1) * 128],
                    ident16.ap()))
            em.task("tensor", fns, reads=[f"h1t{s}", "i16"], writes=[tbk])
        dst = _ap(h1f, tb * 128, [[32768, 128], [2048, 16], [1, 128]])
        em.task("scalar",
                [lambda q, dst=dst: q.copy(dst, b2_bf[:, 2048:4096])],
                reads=["pC", "pZ"], writes=[f"h1_{tb}_0", f"h1_{tb}_1"])
        vback.append(tb)

    while vback:
        vproj(vback.pop(0))
    for m in range(4):
        kproj(m, 1)

    stage_wq(0)
    stage_wq(1)
    for me in range(16):
        r = me % 4
        for ts in range(2):
            bk, bkk = bank512()
            fns = []
            for c in range(16):
                fns.append(lambda t, r=r, ts=ts, c=c, bk=bk: t.matmul(
                    bk, wqr[r].ap()[:, c * 128:(c + 1) * 128],
                    h1f.ap()[:, c * 2048 + 1024 + ts * 512:c * 2048 + 1024 + (ts + 1) * 512],
                    start=(c == 0), stop=(c == 15), skip_group_check=True))
            rd = [f"wqr{r}"] + [f"h1_{8 + 4 * ts + i}_{g}" for i in range(4) for g in range(2)]
            em.task("tensor", fns, reads=rd, writes=[bkk])
            em.task("scalar",
                    [lambda q, me=me, ts=ts, bk=bk: q.copy(
                        qf.ap()[:, me * 1024 + ts * 512:me * 1024 + (ts + 1) * 512], bk)],
                    reads=[bkk], writes=[f"q_{me}_{ts}"])
        if me + 2 < 16:
            stage_wq(me + 2)

    em.task("sync", [lambda s: s.dma_start(out=msk.ap(), in_=mk[0, :, :])],
            writes=["masks"], war=WK + [f"wk{m}_half" for m in range(4)], stream="msk")

    for c in range(16):
        for h in range(2):
            stage(wo[c * 128:(c + 1) * 128, 1024 * h:1024 * (h + 1)],
                  wo_bf.ap()[:, c * 2048 + 1024 * h:c * 2048 + 1024 * (h + 1)],
                  [f"wo{c}_{h}"],
                  war=H1K if (c == 0 and h == 0) else [])

    first_exp = [True]
    YB = [(pc, "pC"), (PB[0], "pB0")]
    ZB = [(pz, "pZ"), (PB[1], "pB1")]
    ZROW = [zinv, zrow2]
    ZBB = [zb_sb, zb2_sb]

    def front_batch(qs, hh, bi):
        kl = KL[qs]
        kv = hh // 4
        st0, ln = BATCHES[qs][bi]
        fns = []
        bkeys = []
        for j in range(ln):
            s_ = st0 + j
            kb = kl[s_]
            bkeys.append(PAK[j])
            fns.append(lambda t, kb=kb, kv=kv, dst=PA[j], qs=qs, hh=hh:
                       t.matmul(dst,
                                kf.ap()[:, kv * 2048 + kb * 128:kv * 2048 + (kb + 1) * 128],
                                qf.ap()[:, hh * 1024 + qs * 512:hh * 1024 + (qs + 1) * 512],
                                start=True, stop=True, skip_group_check=True))
        rd = [f"k_{kv}_{kl[st0 + j] // 4}" for j in range(ln)] + [f"q_{hh}_{qs}"]
        em.task("tensor", fns, reads=rd, writes=bkeys)
        war = (WK + WV + [f"wk{m}_half" for m in range(4)] if first_exp[0] else [])
        first_exp[0] = False
        em.task("scalar",
                [lambda q, src=pa[:, 0:512 * ln], st0=st0, ln=ln: q.activation(
                    pt.ap()[:, st0 * 512:(st0 + ln) * 512], src,
                    AF.Exp, scale=SCALE)],
                reads=bkeys, writes=[f"pt{st0 + j}" for j in range(ln)],
                war=war)
        full = set(range(5, 9)) if qs == 1 else set()
        segs = []
        j = 0
        while j < ln:
            if (st0 + j) in full:
                j += 1
                continue
            j0 = j
            while j < ln and (st0 + j) not in full:
                j += 1
            segs.append((st0 + j0, j - j0))
        if segs:
            em.task("vector",
                    [lambda v, a=a, b=b: v.tensor_tensor(
                        pt.ap()[:, a * 512:(a + b) * 512],
                        pt.ap()[:, a * 512:(a + b) * 512],
                        msk.ap()[:, a * 512:(a + b) * 512], ALU.mult)
                     for (a, b) in segs],
                    reads=["masks"] + [f"pt{st0 + j}" for j in range(ln)],
                    writes=[f"pt{st0 + j}" for j in range(ln)])

    def back_batch(qs, hh, par, bi):
        kl = KL[qs]
        kv = hh // 4
        nk = len(kl)
        st0, ln = BATCHES[qs][bi]
        yb, ybk = YB[par]
        zb, zbk = ZB[par]
        fns = []
        for j in range(ln):
            s_ = st0 + j
            kb = kl[s_]
            first = (s_ == 0)
            last = (s_ == nk - 1)
            fns.append(lambda t, kb=kb, s_=s_, kv=kv, first=first, last=last, yb=yb:
                       t.matmul(yb,
                                vt.ap()[:, kb * 512 + kv * 128:kb * 512 + (kv + 1) * 128],
                                pt.ap()[:, s_ * 512:(s_ + 1) * 512],
                                start=first, stop=last, skip_group_check=True))
        for j in range(ln):
            s_ = st0 + j
            first = (s_ == 0)
            last = (s_ == nk - 1)
            fns.append(lambda t, s_=s_, first=first, last=last, zb=zb:
                       t.matmul(zb[0:1, :], ones_bf.ap(),
                                pt.ap()[:, s_ * 512:(s_ + 1) * 512],
                                start=first, stop=last, skip_group_check=True))
        rd = [f"pt{st0 + j}" for j in range(ln)]
        rd += [f"v_{kl[st0 + j]}" for j in range(ln)] + ["ones_bf", ybk, zbk]
        em.task("tensor", fns, reads=rd, writes=[ybk, zbk])

    def back_norm_pre(qs, hh, par):
        zb, zbk = ZB[par]
        zr = ZROW[par]
        em.task("scalar", [lambda q, zr=zr, zb=zb: q.copy(zr.ap(), zb[0:1, :])],
                reads=[zbk], writes=[f"zrow{par}"])

    def back_norm(qs, hh, par):
        yb, ybk = YB[par]
        zb, zbk = ZB[par]
        zr = ZROW[par]
        zbb = ZBB[par]
        em.task("tensor",
                [lambda t, zb=zb, zr=zr: t.matmul(zb[:, :], ones_row.ap(), zr.ap(),
                                                  start=True, stop=True,
                                                  skip_group_check=True)],
                reads=[f"zrow{par}", "ones_row", zbk], writes=[zbk])
        em.task("scalar", [lambda q, zbb=zbb, zb=zb: q.activation(
            zbb.ap(), zb[:, :], AF.Ln, scale=1.0)],
                reads=[zbk], writes=[f"zb{par}"])
        em.task("scalar", [lambda q, zbb=zbb: q.activation(
            zbb.ap(), zbb.ap(), AF.Exp, scale=-1.0)],
                reads=[f"zb{par}"], writes=[f"zb{par}"])
        war = WQR if (qs == 0 and hh == 0) else []
        em.task("vector",
                [lambda v, hh=hh, qs=qs, yb=yb, zbb=zbb: v.tensor_tensor(
                    yf.ap()[:, hh * 1024 + qs * 512:hh * 1024 + (qs + 1) * 512],
                    yb[:, :], zbb.ap(), ALU.mult)],
                reads=[ybk, f"zb{par}"], writes=[f"y_{hh}_{qs}"], war=war)

    HSEQ = [(qs, hh) for qs in range(2) for hh in range(16)]
    for i, (qs, hh) in enumerate(HSEQ):
        if qs == 1 and hh == 0:
            em.task("sync", [lambda s: s.dma_start(out=msk.ap(), in_=mk[1, :, :])],
                    writes=["masks"], stream="msk")
        nbf = len(BATCHES[qs])
        nbb = len(BATCHES[HSEQ[i - 1][0]]) if i >= 1 else 0
        for bi in range(max(nbf, nbb)):
            if i >= 1 and bi < nbb:
                back_batch(HSEQ[i - 1][0], HSEQ[i - 1][1], (i - 1) % 2, bi)
            if bi < nbf:
                front_batch(qs, hh, bi)
            if bi == 0 and i >= 2:
                back_norm(HSEQ[i - 2][0], HSEQ[i - 2][1], (i - 2) % 2)
        if i >= 1:
            back_norm_pre(HSEQ[i - 1][0], HSEQ[i - 1][1], (i - 1) % 2)
    n = len(HSEQ)
    back_norm(HSEQ[n - 2][0], HSEQ[n - 2][1], (n - 2) % 2)
    for bi in range(len(BATCHES[HSEQ[n - 1][0]])):
        back_batch(HSEQ[n - 1][0], HSEQ[n - 1][1], (n - 1) % 2, bi)
    back_norm_pre(HSEQ[n - 1][0], HSEQ[n - 1][1], (n - 1) % 2)
    back_norm(HSEQ[n - 1][0], HSEQ[n - 1][1], (n - 1) % 2)

    pend_c = []
    for tb in range(8):
        qs = tb // 4
        s = tb % 2
        par = tb % 2
        sx = STT[par][4]
        x2b = x2t if s == 0 else outT
        xk = f"x2t_{s}"
        em.task("gpsimd",
                [lambda q, tb=tb, s=s: q.dma_start(
                    out=xo[s].ap(), in_=x_kv[(8 + tb) * 128:(9 + tb) * 128, :])],
                writes=[f"xo{s}"], war=(KK if tb < 2 else []), stream=f"xo{s}")
        if tb % 2 == 0:
            banks = [(PA[i], PAK[i]) for i in range(4)]
            tp_bf = pa_bf
            tpk = ["pA0", "pA1"]
        else:
            banks = [(PB[0], PBK[0]), (PB[1], PBK[1]), (pc, "pC"), (pz, "pZ")]
            tp_bf = pb_bf
            tpk = ["pB0", "pB1"]
        fns = []
        for c in range(16):
            for ms in range(4):
                fns.append(lambda t, c=c, ms=ms, tb=tb, bk=banks[ms][0]: t.matmul(
                    bk, yf.ap()[:, c * 1024 + tb * 128:c * 1024 + (tb + 1) * 128],
                    wo_bf.ap()[:, c * 2048 + ms * 512:c * 2048 + (ms + 1) * 512],
                    start=(c == 0), stop=(c == 15), skip_group_check=True))
        rd = [f"y_{c}_{qs}" for c in range(16)] + WO
        em.task("tensor", fns, reads=rd, writes=[bk for _, bk in banks])
        if pend_c:
            for f in pend_c.pop(0):
                f()
        ln_stats([(bk, 512) for bk, _ in banks], [bk for _, bk in banks], 4, par,
                 emit_nmr=True)
        whole = pa if tb % 2 == 0 else pb2
        em.task("scalar",
                [lambda q, whole=whole, par=par, x2b=x2b: q.activation(
                    x2b.ap(), whole, AF.Identity,
                    bias=STT[par][3].ap(), scale=STT[par][2].ap())],
                reads=[bk for _, bk in banks] + ["nmr" + sx, "rstd" + sx],
                writes=[f"x2p_{s}"],
                war=[xk] + (VK if tb < 2 else []))
        em.task("vector",
                [lambda v, s=s, x2b=x2b: v.tensor_tensor(x2b.ap(), x2b.ap(),
                                                         xo[s].ap(), ALU.add)],
                reads=[f"xo{s}", f"x2p_{s}"], writes=[xk])
        em.task("gpsimd",
                [lambda q, tb=tb, x2b=x2b: q.dma_start(
                    out=x2s[tb * 128:(tb + 1) * 128, :], in_=x2b.ap())],
                reads=[xk], writes=[f"x2s_{tb}"], stream=f"x2w{s}")
        ln_stats([(x2b.ap(), 2048)], [xk], 4, par, emit_nmr=True)
        em.task("scalar",
                [lambda q, par=par, x2b=x2b: q.activation(
                    h2tok.ap(), x2b.ap(), AF.Identity,
                    bias=STT[par][3].ap(), scale=STT[par][2].ap())],
                reads=[xk, "nmr" + sx, "rstd" + sx], writes=["h2tok"],
                war=((VK + PTK + WV) if tb == 0 else []))

        def emit_tp(tb=tb, tp_bf=tp_bf, tpk=tpk):
            fns = []
            for c in range(16):
                fns.append(lambda t, c=c, tp_bf=tp_bf: t.transpose(
                    tp_bf[:, c * 128:(c + 1) * 128],
                    h2tok.ap()[:, c * 128:(c + 1) * 128], ident16.ap()))
            em.task("tensor", fns, reads=["h2tok", "i16"], writes=tpk)

        def emit_ev(tb=tb, tp_bf=tp_bf, tpk=tpk):
            dst = _ap(h2f, tb * 128, [[16384, 128], [1024, 16], [1, 128]])
            em.task("scalar",
                    [lambda q, dst=dst, tp_bf=tp_bf: q.copy(dst, tp_bf[:, 0:2048])],
                    reads=tpk, writes=[f"h2_{tb}"],
                    war=(QK_ if tb == 0 else []))
        pend_c.append([emit_tp, emit_ev])
    while pend_c:
        for f in pend_c.pop(0):
            f()

    went = [0]
    wbf_used = set()

    def wslot():
        r = went[0] % 5
        went[0] += 1
        return r

    for q in range(4):
        for jj in range(16):
            j = q * 16 + jj
            rg = wslot()
            ru = wslot()
            war_g = (PTK + ["masks"] if rg not in wbf_used else [])
            wbf_used.add(rg)
            for h in range(2):
                stage(wgu[2 * j, :, 1024 * h:1024 * (h + 1)],
                      wbf[rg].ap()[:, 1024 * h:1024 * (h + 1)],
                      [f"wbf{rg}"] if h == 1 else [f"wbf{rg}_h"],
                      war=war_g + ([f"wbf{rg}"] if h == 0 else []), eng="scalar")
            war_u = (PTK + ["masks"] if ru not in wbf_used else [])
            wbf_used.add(ru)
            for h in range(2):
                stage(wgu[2 * j + 1, :, 1024 * h:1024 * (h + 1)],
                      wbf[ru].ap()[:, 1024 * h:1024 * (h + 1)],
                      [f"wbf{ru}"] if h == 1 else [f"wbf{ru}_h"],
                      war=war_u + ([f"wbf{ru}"] if h == 0 else []), eng="scalar")
            fns = []
            for c in range(16):
                for ts in range(2):
                    fns.append(lambda t, c=c, ts=ts, rg=rg: t.matmul(
                        PB[ts], wbf[rg].ap()[:, c * 128:(c + 1) * 128],
                        h2f.ap()[:, c * 1024 + ts * 512:c * 1024 + (ts + 1) * 512],
                        start=(c == 0), stop=(c == 15), skip_group_check=True))
            em.task("tensor", fns, reads=[f"wbf{rg}"] + H2K, writes=PBK)
            fns = []
            for c in range(16):
                for ts in range(2):
                    fns.append(lambda t, c=c, ts=ts, ru=ru: t.matmul(
                        pa[:, ts * 512:(ts + 1) * 512],
                        wbf[ru].ap()[:, c * 128:(c + 1) * 128],
                        h2f.ap()[:, c * 1024 + ts * 512:c * 1024 + (ts + 1) * 512],
                        start=(c == 0), stop=(c == 15), skip_group_check=True))
            em.task("tensor", fns, reads=[f"wbf{ru}"] + H2K,
                    writes=["pA0", "pA1"])
            em.task("scalar",
                    [lambda q_: q_.activation(silu_t.ap(), pb[:, :], AF.Silu)],
                    reads=PBK, writes=["silu"],
                    war=((PTK + ["h2tok"]) if (q == 0 and jj == 0) else []))
            em.task("vector",
                    [lambda v, jj=jj: v.tensor_tensor(
                        a_sb.ap()[:, jj * 1024:(jj + 1) * 1024],
                        silu_t.ap(), pa[:, 0:1024], ALU.mult)],
                    reads=["silu", "pA0", "pA1"], writes=[f"a_{jj}"],
                    war=(YK if (q == 0 and jj == 0) else []))
        for mc in range(16):
            r = wslot()
            for h in range(2):
                stage(wd[q * 16 + mc, :, 1024 * h:1024 * (h + 1)],
                      wbf[r].ap()[:, 1024 * h:1024 * (h + 1)],
                      [f"wbf{r}"] if h == 1 else [f"wbf{r}_h"],
                      war=([f"wbf{r}"] if h == 0 else []), eng="scalar")
            if mc % 2 == 0:
                dps = [pa[:, 1024:1536], pa[:, 1536:2048]]
                dpk = ["pA2", "pA3"]
            else:
                dps = [pz, pc]
                dpk = ["pZ", "pC"]
            fns = []
            for ic in range(16):
                for ts in range(2):
                    fns.append(lambda t, ic=ic, ts=ts, r=r, dps=dps: t.matmul(
                        dps[ts], wbf[r].ap()[:, ic * 128:(ic + 1) * 128],
                        a_sb.ap()[:, ic * 1024 + ts * 512:ic * 1024 + (ts + 1) * 512],
                        start=(ic == 0), stop=(ic == 15), skip_group_check=True))
            em.task("tensor", fns, reads=[f"wbf{r}"] + AK, writes=dpk)
            if q == 0:
                war = WO if mc == 0 else []
                em.task("vector",
                        [lambda s_, mc=mc, dps=dps: s_.tensor_copy(
                            out_fm.ap()[:, mc * 1024:mc * 1024 + 512], dps[0]),
                         lambda s_, mc=mc, dps=dps: s_.tensor_copy(
                            out_fm.ap()[:, mc * 1024 + 512:(mc + 1) * 1024], dps[1])],
                        reads=dpk, writes=[f"of_{mc}"], war=war)
            else:
                em.task("vector",
                        [lambda v, mc=mc, dps=dps: v.tensor_tensor(
                            out_fm.ap()[:, mc * 1024:mc * 1024 + 512],
                            out_fm.ap()[:, mc * 1024:mc * 1024 + 512],
                            dps[0], ALU.add),
                         lambda v, mc=mc, dps=dps: v.tensor_tensor(
                            out_fm.ap()[:, mc * 1024 + 512:(mc + 1) * 1024],
                            out_fm.ap()[:, mc * 1024 + 512:(mc + 1) * 1024],
                            dps[1], ALU.add)],
                        reads=dpk + [f"of_{mc}"], writes=[f"of_{mc}"])

    em.task("sync", [], reads=[f"x2s_{t}" for t in range(8)])

    def e_stage1(tb):
        s = tb % 2
        par = tb % 2
        sx = STT[par][4]
        if s == 1:
            TPP = [(pb2[:, 0:512], "pB0"), (pb2[:, 512:1024], "pB1"),
                   (pb2[:, 1024:1536], "pC"), (pb2[:, 1536:2048], "pZ")]
            tsrc, tkeys = pb2, ["pB0", "pB1", "pC", "pZ"]
        else:
            TPP = [(pa[:, 512 * g:512 * (g + 1)], PAK[g]) for g in range(4)]
            tsrc, tkeys = pa, list(PAK)
        x2buf = x2t if s == 0 else xo[0]
        x2key = "x2t" if s == 0 else "xo0"
        otbuf = outT if s == 0 else xo[1]
        otkey = "outT" if s == 0 else "xo1"
        war_r = [x2key + "fin"]
        if s == 0:
            war_r += ["x2t_0", "x2p_0", "h2tok"]
        em.task("sync",
                [lambda q, tb=tb, x2buf=x2buf: q.dma_start(
                    out=x2buf.ap(), in_=x2s[tb * 128:(tb + 1) * 128, :])],
                reads=[f"x2s_{tb}"], writes=[x2key], war=war_r, stream=f"x2r{s}")
        for g in range(4):
            tpp, tppk = TPP[g]
            fns = []
            for k in range(4):
                mc = g * 4 + k
                fns.append(lambda t, mc=mc, k=k, tb=tb, tpp=tpp: t.transpose(
                    tpp[:, k * 128:(k + 1) * 128],
                    out_fm.ap()[:, mc * 1024 + tb * 128:mc * 1024 + (tb + 1) * 128],
                    ident32.ap()))
            em.task("tensor", fns,
                    reads=[f"of_{g * 4 + k}" for k in range(4)] + ["i32"],
                    writes=[tppk])
        b6, b2_, rs, nm, _ = STT[par]
        fns = []
        for j in range(4):
            fns.append(lambda v, j=j, b6=b6, tsrc=tsrc: v.bn_stats(
                b6.ap()[:, 6 * j:6 * j + 6],
                tsrc[:, 512 * j:512 * (j + 1)]))
        em.task("vector", fns, reads=tkeys, writes=["bn6" + sx])
        em.task("vector", [lambda v, b6=b6, b2_=b2_: v.bn_aggr(
            b2_.ap(), b6.ap()[:, 0:24])],
                reads=["bn6" + sx], writes=["bn2" + sx])

    def e_stage2(tb):
        s = tb % 2
        par = tb % 2
        sx = STT[par][4]
        x2buf = x2t if s == 0 else xo[0]
        x2key = "x2t" if s == 0 else "xo0"
        otbuf = outT if s == 0 else xo[1]
        otkey = "outT" if s == 0 else "xo1"
        if s == 1:
            tsrc, tkeys = pb2, ["pB0", "pB1", "pC", "pZ"]
        else:
            tsrc, tkeys = pa, list(PAK)
        b6, b2_, rs, nm, _ = STT[par]
        em.task("scalar", [lambda q, b2_=b2_, rs=rs: q.activation(
            rs.ap(), b2_.ap()[:, 1:2], AF.Ln, bias=eps_sb.ap(), scale=1.0)],
                reads=["bn2" + sx, "eps"], writes=["rstd" + sx])
        em.task("scalar", [lambda q, rs=rs: q.activation(
            rs.ap(), rs.ap(), AF.Exp, scale=-0.5)],
                reads=["rstd" + sx], writes=["rstd" + sx])
        em.task("vector", [lambda v, b2_=b2_, rs=rs, nm=nm: v.tensor_scalar(
            nm.ap(), b2_.ap()[:, 0:1], rs.ap(), -1.0, ALU.mult, ALU.mult)],
                reads=["bn2" + sx, "rstd" + sx], writes=["nmr" + sx])
        war_o = [otkey + "ln"] + (["xo1"] if s == 1 else
                                  ["h2tok", "x2t_1", "x2p_1"])
        em.task("scalar",
                [lambda q, otbuf=otbuf, par=par, tsrc=tsrc: q.activation(
                    otbuf.ap(), tsrc, AF.Identity,
                    bias=STT[par][3].ap(), scale=STT[par][2].ap())],
                reads=tkeys + ["nmr" + sx, "rstd" + sx],
                writes=[otkey + "ln"], war=war_o)
        em.task("vector",
                [lambda v, x2buf=x2buf, otbuf=otbuf: v.tensor_tensor(
                    x2buf.ap(), x2buf.ap(), otbuf.ap(), ALU.add)],
                reads=[x2key, otkey + "ln"], writes=[x2key + "fin"])
        em.task("sync",
                [lambda q, tb=tb, x2buf=x2buf: q.dma_start(
                    out=x_out[tb * 128:(tb + 1) * 128, :], in_=x2buf.ap())],
                reads=[x2key + "fin"], writes=[x2key], stream=f"out{tb}")

    for tb in range(8):
        e_stage1(tb)
        if tb >= 1:
            e_stage2(tb - 1)
    e_stage2(7)

    for tb in range(8):
        em.wait_only("scalar", f"out{tb}", 1)
        em.wait_only("sync", f"out{tb}", 1)

    sems = {}
    sem_names = list(ENG) + sorted(em.streams)
    for s in sem_names:
        sems[s] = nc.alloc_semaphore(name=f"s_{s}")

    with nc.Block() as block:
        @block.sync
        def _(e):
            em.replay(e, "sync", sems)

        @block.tensor
        def _(e):
            em.replay(e, "tensor", sems)

        @block.vector
        def _(e):
            em.replay(e, "vector", sems)

        @block.scalar
        def _(e):
            em.replay(e, "scalar", sems)

        @block.gpsimd
        def _(e):
            em.replay(e, "gpsimd", sems)

    return nc



def _build_mask_tiles():
    T = 2048
    i = np.arange(T)[:, None]
    j = np.arange(T)[None, :]
    causal = j <= i
    band = j >= i - (WINDOW - 1)
    m = causal & band
    m = m | (j < SINK)
    m = m & ((i >= SINK) | causal)
    out = np.zeros((2, 2, 128, 13 * 512), dtype=BF)
    for par in range(2):
        off = 1024 if par == 0 else 0
        F = np.zeros((2048, 2048), dtype=bool)
        if par == 0:
            F[1024:, 1024:] = m[:1024, :1024]
        else:
            F[1024:, :] = m[1024:, :]
        for qs in range(2):
            q0 = 1024 + qs * 512
            for s_, kb in enumerate(KL[qs]):
                tile = F[q0:q0 + 512, kb * 128:(kb + 1) * 128].T
                out[par, qs, :, s_ * 512:(s_ + 1) * 512] = tile.astype(BF)
    return out


def _prep_weights(Wq, Wk, Wv, Wo, Wgu, Wd):
    wq_pe = np.ascontiguousarray(
        Wq.reshape(16, 128, 16, 128).transpose(0, 3, 2, 1)).reshape(16, 128, 2048)
    wk_pe = np.ascontiguousarray(
        Wk.reshape(4, 128, 16, 128).transpose(0, 3, 2, 1)).reshape(4, 128, 2048)
    wv_t = np.ascontiguousarray(Wv.T)
    wo_t = np.ascontiguousarray(Wo.T)
    gu = np.ascontiguousarray(
        Wgu.reshape(128, 128, 16, 128).transpose(0, 3, 2, 1)).reshape(128, 128, 2048)
    order = np.empty(128, dtype=np.int64)
    order[0::2] = np.arange(64)
    order[1::2] = np.arange(64) + 64
    wgu_pe = np.ascontiguousarray(gu[order])
    wd_pe = np.ascontiguousarray(
        Wd.reshape(16, 128, 4, 16, 128).transpose(2, 0, 4, 3, 1)).reshape(64, 128, 2048)
    return wq_pe, wk_pe, wv_t, wo_t, wgu_pe, wd_pe


def kernel(L0_attn_x, L0_attn_Wq, L0_attn_Wk, L0_attn_Wv, L0_attn_Wo,
           L0_ffn_w_gate_up, L0_ffn_w_down):
    x = np.ascontiguousarray(np.asarray(L0_attn_x, dtype=np.float32))
    wq_pe, wk_pe, wv_t, wo_t, wgu_pe, wd_pe = _prep_weights(
        np.asarray(L0_attn_Wq, dtype=np.float32),
        np.asarray(L0_attn_Wk, dtype=np.float32),
        np.asarray(L0_attn_Wv, dtype=np.float32),
        np.asarray(L0_attn_Wo, dtype=np.float32),
        np.asarray(L0_ffn_w_gate_up, dtype=np.float32),
        np.asarray(L0_ffn_w_down, dtype=np.float32))
    masks = _build_mask_tiles()
    id16 = np.eye(128, dtype=BF)
    id32 = np.eye(128, dtype=np.float32)

    nc = build_nc()

    in_maps = []
    for core in range(NCORE):
        seq, half = core // 2, core % 2
        if half == 1:
            xkv = x[seq]
        else:
            xkv = np.concatenate(
                [np.zeros((1024, 2048), dtype=np.float32), x[seq, :1024]], axis=0)
        par = half
        in_maps.append({
            "x_kv": np.ascontiguousarray(xkv),
            "wq_pe": wq_pe, "wk_pe": wk_pe, "wv_t": wv_t, "wo_t": wo_t,
            "wgu_pe": wgu_pe, "wd_pe": wd_pe,
            "masks": np.ascontiguousarray(masks[par]).reshape(2, 128, 13 * 512),
            "ident16": id16, "ident32": id32,
        })

    res = run_bass_kernel_spmd(nc, in_maps, core_ids=list(range(NCORE)))

    out = np.empty((4, 2048, 2048), dtype=np.float32)
    for core in range(NCORE):
        seq, half = core // 2, core % 2
        out[seq, half * 1024:(half + 1) * 1024, :] = res.results[core]["x_out"]
    return out
